# version 01
# baseline (speedup 1.0000x reference)
import numpy as np
import concourse.bass as bass
import concourse.mybir as mybir

F32 = mybir.dt.float32
BF16 = mybir.dt.bfloat16
AF = mybir.ActivationFunctionType
ALU = mybir.AluOpType
AX = mybir.AxisListType


class Sem:
    def __init__(self, nc, name):
        self.h = nc.alloc_semaphore(name=name)
        self.n = 0

    def inc(self, instr, k=1):
        instr.then_inc(self.h, k)
        self.n += k
        return self.n


class Ctx:
    def __init__(self, nc, blk):
        self.nc = nc
        self.blk = blk
        self.sems = {}
        self.uid = 0

    def sem(self, name):
        if name not in self.sems:
            self.sems[name] = Sem(self.nc, name)
        return self.sems[name]

    def sb(self, name, shape, dtype):
        return self.nc.alloc_sbuf_tensor(name, shape, dtype).ap()

    def ps(self, name, shape, dtype=F32):
        return self.nc.alloc_psum_tensor(name, shape, dtype).ap()


def emit_init(ctx):
    blk = ctx.blk
    ctx.ident = ctx.sb("c_ident", [128, 128], BF16)
    ctx.identf = ctx.sb("c_identf", [128, 128], F32)
    ctx.ones_col = ctx.sb("c_ones_col", [128, 1], BF16)
    fiota = ctx.sb("c_fiota", [128, 128], F32)
    piota = ctx.sb("c_piota", [128, 1], F32)
    ctx.piota_ap = piota
    s = ctx.sem("init")

    @blk.gpsimd
    def _(e):
        e.memset(ctx.ones_col[:], 1.0)
        e.iota(fiota[:], [[1, 128]], channel_multiplier=0,
               allow_small_or_imprecise_dtypes=True)
        s.inc(e.iota(piota[:], [[1, 1]], channel_multiplier=1,
                     allow_small_or_imprecise_dtypes=True))
    w = s.n

    @blk.vector
    def _(e):
        e.wait_ge(s.h, w)
        e.tensor_scalar(out=ctx.ident[:], in0=fiota[:], scalar1=piota[:],
                        scalar2=None, op0=ALU.is_equal)
        s.inc(e.tensor_scalar(out=ctx.identf[:], in0=fiota[:], scalar1=piota[:],
                              scalar2=None, op0=ALU.is_equal))
    ctx.init_done = s.n




def emit_psum_clear(ctx, psums):
    blk = ctx.blk
    s = ctx.sem("psclr")

    @blk.tensor
    def _(e):
        e.wait_ge(ctx.sem("init").h, ctx.init_done)
        ins = None
        for p in psums:
            pf = p if p.dtype == F32 else p.bitcast(F32)
            nb = (pf.shape[-1] * 4 + 2047) // 2048
            for b in range(nb):
                ins = e.matmul(pf[0:1, b * 512: b * 512 + 1],
                               ctx.ones_col[0:1, 0:1], ctx.ones_col[0:1, 0:1],
                               start=True, stop=True)
        if ins is not None:
            s.inc(ins)
    ctx.psclr_done = (s, s.n)


def emit_ln_tile(ctx, X, j, D, xhat_dst, wait_list, eps=1e-5, name="ln"):
    blk = ctx.blk
    if not hasattr(ctx, "ln_cols"):
        ctx.ln_cols = ctx.sb("ln_cols", [128, 8], F32)
        ctx.ln_xc = ctx.sb("ln_xc", [128, 1024], F32)
        ctx.ln_sq = ctx.sb("ln_sq", [128, 1024], F32)
    cols, xc = ctx.ln_cols, ctx.ln_xc
    s_d = ctx.sem(name + "_d")
    s_a = ctx.sem(name + "_a")
    xs = X[:, j * D:(j + 1) * D]
    rD = 1.0 / D

    @blk.vector
    def _(e):
        for (s, c) in wait_list:
            e.wait_ge(s.h, c)
        s_d.inc(e.reduce_sum(cols[:, 0:1], xs, axis=AX.X))
        e.wait_ge(s_d.h, s_d.n)
        s_d.inc(e.tensor_scalar(out=cols[:, 2:3], in0=cols[:, 0:1], scalar1=rD,
                                scalar2=None, op0=ALU.mult))
        e.wait_ge(s_d.h, s_d.n)
        s_d.inc(e.tensor_scalar(out=xc[:, 0:D], in0=xs, scalar1=cols[:, 2:3],
                                scalar2=None, op0=ALU.subtract))
    w_xc = s_d.n

    @blk.scalar
    def _(e):
        e.wait_ge(s_d.h, w_xc)
        s_a.inc(e.activation(out=ctx.ln_sq[:, 0:D], in_=xc[:, 0:D], func=AF.Square,
                             accum_out=cols[:, 1:2]))
    w_ssq = s_a.n

    @blk.vector
    def _(e):
        e.wait_ge(s_a.h, w_ssq)
        s_d.inc(e.tensor_scalar(out=cols[:, 3:4], in0=cols[:, 1:2], scalar1=rD,
                                scalar2=eps, op0=ALU.mult, op1=ALU.add))
    w_var = s_d.n

    @blk.scalar
    def _(e):
        e.wait_ge(s_d.h, w_var)
        s_a.inc(e.activation(out=cols[:, 6:7], in_=cols[:, 3:4], func=AF.Sqrt))
    w_std = s_a.n

    @blk.vector
    def _(e):
        e.wait_ge(s_a.h, w_std)
        s_d.inc(e.reciprocal(cols[:, 4:5], cols[:, 6:7]))
        e.wait_ge(s_d.h, s_d.n)
        s_d.inc(e.tensor_scalar(out=xhat_dst, in0=xc[:, 0:D],
                                scalar1=cols[:, 4:5], scalar2=None, op0=ALU.mult))
    return (s_d, s_d.n)


def emit_transpose_128(ctx, src_tile, dst_cb, pt, wait_list, name="tp"):
    blk = ctx.blk
    s_mm = ctx.sem(name + "_mm")
    s_ev = ctx.sem(name + "_ev")
    K = src_tile.shape[-1] // 128
    assert K % 4 == 0
    for b in range(K // 4):
        @blk.tensor
        def _(e, b=b):
            for (s, c) in wait_list:
                e.wait_ge(s.h, c)
            if s_ev.n:
                e.wait_ge(s_ev.h, s_ev.n)
            for q in range(4):
                ins = e.transpose(pt[:, q * 128:(q + 1) * 128],
                                  src_tile[:, (b * 4 + q) * 128:(b * 4 + q + 1) * 128],
                                  ctx.ident[:])
            s_mm.inc(ins)
        w_mm = s_mm.n

        @blk.vector
        def _(e, b=b):
            e.wait_ge(s_mm.h, w_mm)
            s_ev.inc(e.tensor_copy(out=dst_cb(b),
                                   in_=pt[:].rearrange("p (k f) -> p k f", f=128)))
    return (s_ev, s_ev.n)


class L0FFN:

    def __init__(self, ctx, T, D, F, E, dram):
        self.ctx = ctx
        self.T, self.D, self.F, self.E = T, D, F, E
        self.DK, self.FC = D // 128, F // 128
        self.NJ = T // 128
        self.TN = 512
        self.NT = T // self.TN
        self.G = self.TN // 128
        self.dram = dram
        c = ctx
        DK, FC, E_, TN = self.DK, self.FC, E, self.TN
        self.GW = c.sb("f0_GW", [128, DK * E_], BF16)
        self.B1C = c.sb("f0_B1C", [128, E_ * FC], F32)
        self.B2sb = c.sb("f0_B2sb", [E_, D], BF16)
        self.b1row = c.sb("f0_b1row", [FC, 128], F32)
        self.Mall = c.sb("f0_Mall", [128, self.NJ * E_], F32)
        self.Mbf = c.sb("f0_Mbf", [128, E_], BF16)
        self.MT = c.sb("f0_MT", [E_, self.NJ * 128], BF16)
        self.rcols = c.sb("f0_rcols", [128, 8], F32)
        self.Lf = c.sb("f0_Lf", [128, E_], F32)
        self.Ebuf = c.sb("f0_Ebuf", [128, E_], F32)
        self.XH = c.sb("f0_XH", [128, 1024], BF16)
        self.XT = c.sb("f0_XT", [128, 2 * DK * TN], BF16)
        self.W1 = c.sb("f0_W1", [128, 2 * DK * 512], BF16)
        self.NRING = 12
        self.W2 = c.sb("f0_W2", [128, self.NRING * 4 * 512], BF16)
        self.H = c.sb("f0_H", [128, FC * TN], BF16)
        self.pG = [c.ps("f0_pG0", [128, 512]), c.ps("f0_pG1", [128, 512])]
        self.pY = [c.ps("f0_pY0", [128, 512]), c.ps("f0_pY1", [128, 512])]
        self.pR = c.ps("f0_pR", [128, 512])
        self.pT = c.ps("f0_pT", [128, 512], BF16)
        self.gq1 = 0
        self.gq2 = 0
        self.gfc = 0
        self.ggo = 0
        self.gj = 0

    def load_weights(self, wait_init):
        c, blk = self.ctx, self.ctx.blk
        DK, FC, E, D = self.DK, self.FC, self.E, self.D
        s_l = c.sem("f0_wl")
        s_mm = c.sem("f0_wlmm")
        s_dv = c.sem("f0_wldv")
        d = self.dram

        @blk.gpsimd
        def _(e):
            s_l.inc(e.dma_start(out=self.GW[:].rearrange("p (k c) -> p k c", c=E),
                                in_=d["gate_w"].rearrange("(k p) c -> p k c", p=128)), 16)
            s_l.inc(e.dma_start(out=self.B2sb[:, :], in_=d["b2"][:, :]), 16)
        for ei in range(E):
            @blk.sync
            def _(e, ei=ei):
                if ei == 0:
                    pass
                else:
                    e.wait_ge(s_mm.h, ei)
                s_l.inc(e.dma_start(out=self.b1row[:, :],
                                    in_=d["b1"][ei].rearrange("(r c) -> r c", c=128)), 16)
            w_b = s_l.n

            @blk.tensor
            def _(e, ei=ei):
                e.wait_ge(s_l.h, w_b)
                e.wait_ge(c.sem("init").h, wait_init)
                if ei:
                    e.wait_ge(s_dv.h, ei)
                s_mm.inc(e.transpose(self.pR[:, 0:FC], self.b1row[:, :], c.identf[0:FC, 0:FC]))
            w_t = s_mm.n

            @blk.vector
            def _(e, ei=ei):
                e.wait_ge(s_mm.h, w_t)
                s_dv.inc(e.tensor_copy(out=self.B1C[:, ei * FC:(ei + 1) * FC],
                                       in_=self.pR[:, 0:FC]))
        self.w_wload = (s_l.n, s_dv.n)

    def emit(self, tag):
        for n in range(self.NT):
            self._emit_n(n, tag)
        self._emit_b2(tag)

    def _emit_n(self, n, tag):
        c = self.ctx
        blk = c.blk
        T, D, F, E = self.T, self.D, self.F, self.E
        DK, FC, TN, G = self.DK, self.FC, self.TN, self.G
        X = c.X
        nsl = (self.gj // G) % 2
        XTs = self.XT[:, nsl * DK * TN:(nsl + 1) * DK * TN]
        s_stt = c.sem("f0_stt")
        s_g1 = c.sem("f0_g1")
        s_g2 = c.sem("f0_g2")
        s_gel = c.sem("f0_gel")
        s_w1 = c.sem("f0_w1")
        s_w2 = c.sem("f0_w2")

        for g in range(G):
            j = n * G + g
            jj = self.gj
            wl = [(c.sem("f0_xload"), c.xload_count)]
            if hasattr(self, "slot_free") and self.slot_free[nsl] is not None:
                wl.append(self.slot_free[nsl])
            if getattr(self, "xtile_dep", None) is not None:
                wl.append(self.xtile_dep)
            s_tpmm = c.sem("f0_tp_mm")
            if s_tpmm.n:
                wl.append((s_tpmm, s_tpmm.n))
            ln_done = emit_ln_tile(c, X, j, D, self.XH[:, 0:D], wl, name="f0_ln")

            XTr = XTs.rearrange("p (k t) -> p k t", t=TN)

            def dst_cb(b, g=g, XTr=XTr):
                return XTr[:, b * 4:(b + 1) * 4, g * 128:(g + 1) * 128]
            wl_tp = [ln_done]
            s_rdv = c.sem("f0_rdv")
            if s_rdv.n:
                wl_tp.append((s_rdv, s_rdv.n))
            tp_done = emit_transpose_128(c, self.XH[:, 0:D], dst_cb, self.pT,
                                         wl_tp, name="f0_tp")
            self._emit_router(j, XTs, g, tp_done)
            self.gj += 1
        tp_all = (c.sem("f0_tp_ev"), c.sem("f0_tp_ev").n)

        if not hasattr(self, "slot_free"):
            self.slot_free = [None, None]
        for ei in range(E):
            it_first = (ei == 0)
            NQ1 = F // 512
            for q in range(NQ1):
                buf = self.gq1 % 2
                self.gq1 += 1

                @blk.gpsimd
                def _(e, q=q, ei=ei, buf=buf, gq=self.gq1 - 1):
                    if gq >= 2:
                        e.wait_ge(s_g1.h, (gq - 1) * 4)
                    s_w1.inc(e.dma_start(
                        out=self.W1[:, buf * DK * 512:(buf + 1) * DK * 512]
                            .rearrange("p (k f) -> p k f", f=512),
                        in_=self.dram["w1"][ei, :, q * 512:(q + 1) * 512]
                            .rearrange("(k p) f -> p k f", p=128)), 16)
            for oh in range(2):
                for qg in range(FC // 4):
                    slot = self.gq2 % self.NRING
                    self.gq2 += 1

                    @blk.gpsimd
                    def _(e, ei=ei, oh=oh, qg=qg, slot=slot, gq=self.gq2 - 1):
                        if gq >= self.NRING:
                            old_half = (gq - self.NRING) // (FC // 4)
                            e.wait_ge(s_g2.h, (old_half + 1) * G)
                        s_w2.inc(e.dma_start(
                            out=self.W2[:, slot * 4 * 512:(slot + 1) * 4 * 512]
                                .rearrange("p (k f) -> p k f", f=512),
                            in_=self.dram["w2"][ei, qg * 512:(qg + 1) * 512,
                                                oh * 512:(oh + 1) * 512]
                                .rearrange("(k p) f -> p k f", p=128)), 16)

            for fc in range(FC):
                gq = (self.gfc // 4)
                buf = gq % 2
                gp = self.pG[self.gfc % 2]
                gfc = self.gfc
                self.gfc += 1

                @blk.tensor
                def _(e, fc=fc, ei=ei, buf=buf, gp=gp, gfc=gfc, gq=gq, XTs=XTs):
                    e.wait_ge(s_w1.h, (gq + 1) * 16)
                    if gfc >= 2:
                        e.wait_ge(s_gel.h, gfc - 1)
                    if fc == 0 and ei == 0:
                        e.wait_ge(tp_all[0].h, tp_all[1])
                    for dk in range(DK):
                        ins = e.matmul(
                            gp[:],
                            self.W1[:, buf * DK * 512 + dk * 512 + (fc % 4) * 128:
                                    buf * DK * 512 + dk * 512 + (fc % 4) * 128 + 128],
                            XTs[:, dk * TN:(dk + 1) * TN],
                            start=(dk == 0), stop=(dk == DK - 1))
                    s_g1.inc(ins)
                w_g1 = s_g1.n

                @blk.scalar
                def _(e, fc=fc, ei=ei, gfc=gfc, w_g1=w_g1):
                    e.wait_ge(s_g1.h, w_g1)
                    if gfc == 0:
                        e.wait_ge(c.sem("f0_wldv").h, self.w_wload[1])
                    s_gel.inc(e.activation(out=self.H[:, fc * TN:(fc + 1) * TN],
                                           in_=self.pG[gfc % 2][:],
                                           func=AF.Gelu,
                                           bias=self.B1C[:, ei * FC + fc: ei * FC + fc + 1]))

            base_half = (self.gq2 - 2 * (FC // 4)) // (FC // 4)
            for oh in range(2):
                half_idx = base_half + oh
                for g in range(G):
                    yp = self.pY[self.ggo % 2]
                    jt = n * G + g
                    ggo = self.ggo
                    self.ggo += 1

                    @blk.tensor
                    def _(e, ei=ei, oh=oh, g=g, yp=yp, jt=jt, ggo=ggo, half_idx=half_idx):
                        e.wait_ge(s_gel.h, (self.gfc // FC) * FC)
                        if ggo >= 2:
                            e.wait_ge(s_stt.h, ggo - 1)
                        e.wait_ge(s_w2.h, (half_idx + 1) * (FC // 4) * 16)
                        for fc in range(FC):
                            slot = (half_idx * (FC // 4) + fc // 4) % self.NRING
                            ins = e.matmul(
                                yp[:],
                                self.H[:, fc * TN + g * 128: fc * TN + (g + 1) * 128],
                                self.W2[:, slot * 4 * 512 + (fc % 4) * 512:
                                        slot * 4 * 512 + (fc % 4) * 512 + 512],
                                start=(fc == 0), stop=(fc == FC - 1))
                        s_g2.inc(ins)
                    w_g2 = s_g2.n

                    @blk.vector
                    def _(e, ei=ei, oh=oh, g=g, yp=yp, jt=jt, w_g2=w_g2):
                        e.wait_ge(s_g2.h, w_g2)
                        s_stt.inc(e.scalar_tensor_tensor(
                            out=c.X[:, jt * self.D + oh * 512: jt * self.D + oh * 512 + 512],
                            in0=yp[:],
                            scalar=self.Mall[:, jt * E + ei: jt * E + ei + 1],
                            in1=c.X[:, jt * self.D + oh * 512: jt * self.D + oh * 512 + 512],
                            op0=ALU.mult, op1=ALU.add))
        self.slot_free[nsl] = (s_g1, s_g1.n)

    def _emit_router(self, j, XTs, g, tp_done):
        c = self.ctx
        blk = c.blk
        E, DK, TN, FC = self.E, self.DK, self.TN, self.FC
        s_rmm = c.sem("f0_rmm")
        s_rdv = c.sem("f0_rdv")
        s_rac = c.sem("f0_rac")
        pR = self.pR

        @blk.tensor
        def _(e):
            e.wait_ge(tp_done[0].h, tp_done[1])
            e.wait_ge(c.sem("f0_wl").h, 16)
            e.wait_ge(c.sem("f0_wldv").h, self.w_wload[1])
            if s_rdv.n:
                e.wait_ge(s_rdv.h, s_rdv.n)
            for dk in range(DK):
                ins = e.matmul(pR[:, 0:E],
                               XTs[:, dk * TN + g * 128: dk * TN + (g + 1) * 128],
                               self.GW[:, dk * E:(dk + 1) * E],
                               start=(dk == 0), stop=(dk == DK - 1))
            s_rmm.inc(ins)
        w_r = s_rmm.n

        @blk.vector
        def _(e):
            e.wait_ge(s_rmm.h, w_r)
            s_rdv.inc(e.tensor_copy(out=self.Lf[:], in_=pR[:, 0:E]))
            e.wait_ge(s_rdv.h, s_rdv.n)
            s_rdv.inc(e.reduce_max(self.rcols[:, 0:1], self.Lf[:], axis=AX.X, negate=True))
        w_m = s_rdv.n

        @blk.scalar
        def _(e):
            e.wait_ge(s_rdv.h, w_m)
            s_rac.inc(e.activation(out=self.Ebuf[:], in_=self.Lf[:], func=AF.Exp,
                                   bias=self.rcols[:, 0:1], accum_out=self.rcols[:, 1:2]))
        w_e = s_rac.n

        @blk.vector
        def _(e):
            e.wait_ge(s_rac.h, w_e)
            s_rdv.inc(e.reciprocal(self.rcols[:, 2:3], self.rcols[:, 1:2]))
            e.tensor_scalar(out=self.Lf[:], in0=self.Lf[:], scalar1=self.rcols[:, 0:1],
                            scalar2=None, op0=ALU.add)
            e.wait_ge(s_rdv.h, s_rdv.n)
            s_rdv.inc(e.tensor_scalar(out=self.rcols[:, 3:4], in0=self.rcols[:, 2:3],
                                      scalar1=0.5, scalar2=None, op0=ALU.mult))
            e.wait_ge(s_rdv.h, s_rdv.n)
            s_rdv.inc(e.tensor_scalar(out=self.Mall[:, j * E:(j + 1) * E], in0=self.Lf[:],
                                      scalar1=0.0, scalar2=self.rcols[:, 3:4],
                                      op0=ALU.is_equal, op1=ALU.mult))
            e.wait_ge(s_rdv.h, s_rdv.n)
            s_rdv.inc(e.tensor_copy(out=self.Mbf[:], in_=self.Mall[:, j * E:(j + 1) * E]))
        w_mb = s_rdv.n

        @blk.tensor
        def _(e):
            e.wait_ge(s_rdv.h, w_mb)
            s_rmm.inc(e.transpose(self.pT[0:E, 0:128], self.Mbf[:], c.ident[:]))
        w_mt = s_rmm.n

        @blk.vector
        def _(e):
            e.wait_ge(s_rmm.h, w_mt)
            s_rdv.inc(e.tensor_copy(out=self.MT[0:E, j * 128:(j + 1) * 128],
                                    in_=self.pT[0:E, 0:128]))

    def _emit_b2(self, tag):
        c = self.ctx
        blk = c.blk
        E, D = self.E, self.D
        s_stt = c.sem("f0_stt")
        s_mm = c.sem("f0_b2mm")
        s_dv = c.sem("f0_b2dv")
        stt_all = s_stt.n
        for j in range(self.NJ):
            for oh in range(2):
                yp = self.pY[self.ggo % 2]
                self.ggo += 1

                @blk.tensor
                def _(e, j=j, oh=oh, yp=yp):
                    e.wait_ge(s_stt.h, stt_all)
                    e.wait_ge(c.sem("f0_wl").h, 32)
                    if s_mm.n >= 2:
                        e.wait_ge(s_dv.h, s_mm.n - 1)
                    s_mm.inc(e.matmul(yp[:], self.MT[0:E, j * 128:(j + 1) * 128],
                                      self.B2sb[0:E, oh * 512:(oh + 1) * 512],
                                      start=True, stop=True))
                w_b2 = s_mm.n

                @blk.vector
                def _(e, j=j, oh=oh, yp=yp, w_b2=w_b2):
                    e.wait_ge(s_mm.h, w_b2)
                    s_dv.inc(e.tensor_add(
                        c.X[:, j * D + oh * 512: j * D + (oh + 1) * 512],
                        c.X[:, j * D + oh * 512: j * D + (oh + 1) * 512],
                        yp[:]))
        self.xtile_dep = (s_dv, s_dv.n)
        self.done = (s_dv, s_dv.n)


def build_l0ffn_program(T=2048, D=1024, F=4096, E=8, debug=False):
    nc = bass.Bass("TRN2", target_bir_lowering=False)
    x_dram = nc.dram_tensor("x", [T, D], F32, kind="ExternalInput").ap()
    dram = {
        "gate_w": nc.dram_tensor("gate_w", [D, E], F32, kind="ExternalInput").ap(),
        "w1": nc.dram_tensor("w1", [E, D, F], F32, kind="ExternalInput").ap(),
        "b1": nc.dram_tensor("b1", [E, F], F32, kind="ExternalInput").ap(),
        "w2": nc.dram_tensor("w2", [E, F, D], F32, kind="ExternalInput").ap(),
        "b2": nc.dram_tensor("b2", [E, D], F32, kind="ExternalInput").ap(),
    }
    y_dram = nc.dram_tensor("y", [T, D], F32, kind="ExternalOutput").ap()

    with nc.Block() as blk:
        ctx = Ctx(nc, blk)
        emit_init(ctx)
        ctx.X = ctx.sb("X", [128, (T // 128) * D], F32)
        s_x = ctx.sem("f0_xload")

        @blk.sync
        def _(e):
            s_x.inc(e.dma_start(
                out=ctx.X[:].rearrange("p (j d) -> p j d", d=D),
                in_=x_dram.rearrange("(j p) d -> p j d", p=128)), 16)
        ctx.xload_count = s_x.n

        ffn = L0FFN(ctx, T, D, F, E, dram)
        emit_psum_clear(ctx, [ffn.pG[0], ffn.pG[1], ffn.pY[0], ffn.pY[1], ffn.pR, ffn.pT])
        ffn.load_weights(ctx.init_done)
        ffn.emit("a")

        if debug:
            FC = F // 128
            s_dbg = ctx.sem("dbg")
            dbgs = [
                ("d_mall", [128, (T // 128) * E], ffn.Mall[:]),
                ("d_b1c", [128, E * FC], ffn.B1C[:]),
                ("d_h", [128, FC * 512], ffn.H[:]),
                ("d_xt", [128, 2 * (D // 128) * 512], ffn.XT[:]),
                ("d_rc", [128, 8], ffn.rcols[:]),
                ("d_lf", [128, E], ffn.Lf[:]),
                ("d_eb", [128, E], ffn.Ebuf[:]),
            ]

            @blk.gpsimd
            def _(e):
                e.wait_ge(ffn.done[0].h, ffn.done[1])
                for (name, shp, src) in dbgs:
                    dst = nc.dram_tensor(name, shp, F32, kind="ExternalOutput").ap()
                    s_dbg.inc(e.dma_start(out=dst[:, :], in_=src), 16)
                e.wait_ge(s_dbg.h, s_dbg.n)

        s_out = ctx.sem("out")

        @blk.sync
        def _(e):
            e.wait_ge(ffn.done[0].h, ffn.done[1])
            s_out.inc(e.dma_start(
                out=y_dram.rearrange("(j p) d -> p j d", p=128),
                in_=ctx.X[:].rearrange("p (j d) -> p j d", d=D)), 16)

        @blk.sync
        def _(e):
            e.wait_ge(s_out.h, s_out.n)

    return nc


class L0Attn:

    def __init__(self, ctx, T, D, dram, rg):
        self.ctx = ctx
        self.T, self.D = T, D
        self.DK = D // 128
        self.NJ = T // 128
        self.dram = dram
        self.rg = rg
        c = ctx
        nc = c.nc
        NJ = self.NJ
        self.XHAT = c.sb("a0_XHAT", [128, NJ * D], BF16)
        self.XT = c.sb("a0_XT", [128, self.DK * T], BF16)
        self.Wqs = c.sb("a0_Wqs", [64, 64], BF16)
        self.Wks = c.sb("a0_Wks", [64, 64], BF16)
        self.Wvs = c.sb("a0_Wvs", [64, 64], BF16)
        self.Gs = c.sb("a0_Gs", [128, 64], BF16)
        self.WvTs = c.sb("a0_WvTs", [128, 64], BF16)
        self.OHi = c.sb("a0_OHi", [16, 2048], F32)
        self.XS = c.sb("a0_XS", [128, 1024], F32)
        self.XSgb = c.sb("a0_XSgb", [128, 1024], BF16)
        self.XST = c.sb("a0_XST", [128, 16], BF16)
        self.V16 = c.sb("a0_V16", [16, 128], BF16)
        self.OH = c.sb("a0_OH", [16, 16 * 128], BF16)
        self.DS = c.sb("a0_DS", [128, NJ * 16], F32)
        self.PD = c.sb("a0_PD", [128, (NJ // 2) * 16], F32)
        self.P = c.sb("a0_P", [128, NJ * 16], F32)
        self.prod = c.sb("a0_prod", [128, 2 * 64], F32)
        self.SC32 = c.sb("a0_SC32", [128, 1], F32)
        self.pZ = [c.ps("a0_pZ0", [128, 512]), c.ps("a0_pZ1", [128, 512])]
        self.pXs = c.ps("a0_pXs", [1, 1024])
        self.pVB = c.ps("a0_pVB", [128, 1024])
        self.pT = c.ps("a0_pT", [128, 512], BF16)
        self.ar_in = nc.dram_tensor("a0_arin", [2, 1024], F32)
        self.ar_out = nc.dram_tensor("a0_arout", [2, 1024], F32)
        self.xtile_dep = None

    def load_weights(self, wait_init):
        c, blk = self.ctx, self.ctx.blk
        s_l = c.sem("a0_wl")
        s_mm = c.sem("a0_wmm")
        s_dv = c.sem("a0_wdv")
        d = self.dram

        @blk.gpsimd
        def _(e):
            s_l.inc(e.dma_start(out=self.Wqs[:, :], in_=d["Wq"][:, :]), 16)
            s_l.inc(e.dma_start(out=self.Wks[:, :], in_=d["Wk"][:, :]), 16)
            s_l.inc(e.dma_start(out=self.Wvs[:, :], in_=d["Wv"][:, :]), 16)
            e.wait_ge(c.sem("init").h, wait_init)
            e.memset(self.SC32[:], 1.0 / 32.0)
            s_l.inc(e.iota(self.OHi[:, :], [[1, 16], [0, 128]], channel_multiplier=0,
                           allow_small_or_imprecise_dtypes=True))
        w_l = s_l.n

        @blk.vector
        def _(e):
            e.wait_ge(s_l.h, w_l)
            s_dv.inc(e.tensor_scalar(out=self.OH[0:16, :], in0=self.OHi[:, :],
                                     scalar1=c.piota_ap[0:16, :], scalar2=None,
                                     op0=ALU.is_equal))
        w_oh = s_dv.n

        @blk.tensor
        def _(e):
            e.wait_ge(s_l.h, w_l)
            s_mm.inc(e.matmul(self.pZ[0][0:64, 0:64], self.Wqs[:, :], self.Wks[:, :],
                              start=True, stop=True))
            s_mm.inc(e.transpose(self.pT[0:64, 0:64], self.Wvs[:, :], c.ident[0:64, 0:64]))
        w_mm = s_mm.n

        @blk.vector
        def _(e):
            e.wait_ge(s_mm.h, w_mm)
            e.tensor_copy(out=self.Gs[0:64, :], in_=self.pZ[0][0:64, 0:64])
            e.tensor_copy(out=self.Gs[64:128, :], in_=self.pZ[0][0:64, 0:64])
            e.tensor_copy(out=self.WvTs[0:64, :], in_=self.pT[0:64, 0:64])
            s_dv.inc(e.tensor_copy(out=self.WvTs[64:128, :], in_=self.pT[0:64, 0:64]))
        self.w_wload = (s_dv, s_dv.n)

    def _piota(self):
        return self.ctx.piota_ap

    def emit(self, tag):
        c = self.ctx
        blk = c.blk
        T, D, DK, NJ = self.T, self.D, self.DK, self.NJ
        NJH = NJ // 2
        s_xs = c.sem("a0_xs")
        s_cc = c.sem("a0_cc")
        s_dma = c.sem("a0_dma")
        s_z = c.sem("a0_z")
        s_ds = c.sem("a0_ds")
        s_pr = c.sem("a0_pr")
        s_vb = c.sem("a0_vb")
        s_stt = c.sem("a0_stt")
        s_tpmm = c.sem("a0_tp_mm")
        s_tpev = c.sem("a0_tp_ev")

        zr = 0
        z_free = [None, None]

        for j in range(NJ):
            wl = [(c.sem("f0_xload"), c.xload_count)]
            if self.xtile_dep is not None:
                wl.append(self.xtile_dep)
            if s_tpmm.n:
                wl.append((s_tpmm, s_tpmm.n))
            ln_done = emit_ln_tile(c, c.X, j, D, self.XHAT[:, j * D:(j + 1) * D],
                                   wl, name="a0_ln")
            XTr = self.XT.rearrange("p (k t) -> p k t", t=T)
            wl_tp = [ln_done]
            if j == 0:
                wl_tp.append((self.w_wload[0], self.w_wload[1]))
            tp_done = emit_transpose_128(c, self.XHAT[:, j * D:(j + 1) * D],
                                         lambda b, j=j: XTr[:, b * 4:(b + 1) * 4, j * 128:(j + 1) * 128],
                                         self.pT, wl_tp, name="a0_tp")
            b = j // NJH
            first = (j % NJH == 0)
            last = (j % NJH == NJH - 1)

            @blk.tensor
            def _(e, j=j, b=b, first=first, last=last, ln_done=ln_done):
                e.wait_ge(ln_done[0].h, ln_done[1])
                if first and b == 1:
                    e.wait_ge(s_pr.h, 1)
                for h2 in range(2):
                    ins = e.matmul(self.pXs[0:1, h2 * 512:(h2 + 1) * 512],
                                   c.ones_col[:, :],
                                   self.XHAT[:, j * D + h2 * 512: j * D + h2 * 512 + 512],
                                   start=first, stop=last)
                if last:
                    s_xs.inc(ins)
            if last:
                w_xs_last = s_xs.n

                @blk.vector
                def _(e, b=b, w=w_xs_last):
                    e.wait_ge(s_xs.h, w)
                    s_pr.inc(e.tensor_copy(out=self.XS[64 * b:64 * b + 1, :],
                                           in_=self.pXs[0:1, 0:1024]))

            for h in range(16):
                zi = zr % 2
                zr += 1

                @blk.tensor
                def _(e, j=j, h=h, zi=zi, tp_done=tp_done):
                    e.wait_ge(tp_done[0].h, tp_done[1])
                    e.wait_ge(self.w_wload[0].h, self.w_wload[1])
                    if z_free[zi] is not None:
                        e.wait_ge(z_free[zi][0].h, z_free[zi][1])
                    base = 64 * (h % 2)
                    s_z.inc(e.matmul(
                        self.pZ[zi][:, 0:64],
                        self.XT[base:base + 64,
                                (h // 2) * T + j * 128:(h // 2) * T + (j + 1) * 128],
                        self.Gs[base:base + 64, :], start=True, stop=True))
                w_z = s_z.n

                @blk.vector
                def _(e, j=j, h=h, zi=zi, w_z=w_z):
                    e.wait_ge(s_z.h, w_z)
                    s_ds.inc(e.tensor_mul(self.prod[:, zi * 64:zi * 64 + 64],
                                          self.XHAT[:, j * D + h * 64: j * D + (h + 1) * 64],
                                          self.pZ[zi][:, 0:64]))
                    e.wait_ge(s_ds.h, s_ds.n)
                    s_ds.inc(e.reduce_sum(self.DS[:, j * 16 + h: j * 16 + h + 1],
                                          self.prod[:, zi * 64:zi * 64 + 64], axis=AX.X))
                z_free[zi] = (s_ds, s_ds.n)

        w_xs_all = s_pr.n

        @blk.gpsimd
        def _(e):
            e.wait_ge(s_pr.h, w_xs_all)
            s_dma.inc(e.dma_start(out=self.ar_in.ap()[0:1, :], in_=self.XS[0:1, :]), 16)
            s_dma.inc(e.dma_start(out=self.ar_in.ap()[1:2, :], in_=self.XS[64:65, :]), 16)
            e.wait_ge(s_dma.h, s_dma.n)
            s_cc.inc(e.collective_compute(
                "AllReduce", ALU.add, self.rg,
                ins=[self.ar_in.ap().opt()], outs=[self.ar_out.ap().opt()]))
            e.wait_ge(s_cc.h, s_cc.n)
            s_dma.inc(e.dma_start(out=self.XS[0:1, :], in_=self.ar_out.ap()[0:1, :]), 16)
            s_dma.inc(e.dma_start(out=self.XS[64:65, :], in_=self.ar_out.ap()[1:2, :]), 16)
        w_xsg = s_dma.n

        @blk.vector
        def _(e):
            e.wait_ge(s_dma.h, w_xsg)
            e.tensor_copy(out=self.XSgb[0:1, :], in_=self.XS[0:1, :])
            s_vb.inc(e.tensor_copy(out=self.XSgb[64:65, :], in_=self.XS[64:65, :]))
        w_xsgb = s_vb.n

        w_ds_now = s_ds.n

        @blk.tensor
        def _(e):
            e.wait_ge(s_vb.h, w_xsgb)
            e.wait_ge(s_ds.h, w_ds_now)
            for b in range(2):
                for ch in range(8):
                    ins = e.matmul(self.pZ[0][:, b * 8 + ch: b * 8 + ch + 1],
                                   self.XSgb[64 * b:64 * b + 1, ch * 128:(ch + 1) * 128],
                                   c.ones_col[64 * b:64 * b + 1, 0:1], start=True, stop=True)
            s_z.inc(ins)
        w_xst_mm = s_z.n

        @blk.vector
        def _(e):
            e.wait_ge(s_z.h, w_xst_mm)
            s_vb.inc(e.tensor_copy(out=self.XST[:, :], in_=self.pZ[0][:, 0:16]))
        w_xst = s_vb.n

        for par in range(2):
            @blk.tensor
            def _(e, par=par):
                e.wait_ge(s_vb.h, w_xst if par == 0 else w_xst + par)
                s_z.inc(e.matmul(self.pZ[1][0:16, 0:64],
                                 self.XST[64 * par:64 * par + 64, :],
                                 self.WvTs[64 * par:64 * par + 64, :],
                                 start=True, stop=True))
            w_v = s_z.n

            @blk.vector
            def _(e, par=par):
                e.wait_ge(s_z.h, w_v)
                s_vb.inc(e.tensor_copy(out=self.V16[0:16, par * 64:(par + 1) * 64],
                                       in_=self.pZ[1][0:16, 0:64]))
        w_v16 = s_vb.n

        w_ds_all = s_ds.n

        @blk.vector
        def _(e):
            e.wait_ge(s_ds.h, w_ds_all)
            s_pr.inc(e.tensor_sub(self.PD[:, 0:NJH * 16], self.DS[:, 0:NJH * 16],
                                  self.DS[:, NJH * 16:NJ * 16]))
        w_pd = s_pr.n

        @blk.scalar
        def _(e):
            e.wait_ge(s_pr.h, w_pd)
            s_pr.inc(e.activation(out=self.P[:, 0:NJH * 16], in_=self.PD[:, 0:NJH * 16],
                                  func=AF.Sigmoid, scale=self.SC32[:, 0:1]))
        w_p0 = s_pr.n

        @blk.vector
        def _(e):
            e.wait_ge(s_pr.h, w_p0)
            s_pr.inc(e.tensor_scalar(out=self.P[:, NJH * 16:NJ * 16],
                                     in0=self.P[:, 0:NJH * 16],
                                     scalar1=-1.0, scalar2=1.0,
                                     op0=ALU.mult, op1=ALU.add))
        w_p1 = s_pr.n

        for b in range(2):
            @blk.tensor
            def _(e, b=b):
                e.wait_ge(s_vb.h, w_v16)
                if b == 1:
                    e.wait_ge(s_stt.h, NJH * 16)
                for h in range(16):
                    r = b * 8 + h // 2
                    par = h % 2
                    ins = e.matmul(self.pVB[:, h * 64:(h + 1) * 64],
                                   self.OH[0:16, r * 128:(r + 1) * 128],
                                   self.V16[0:16, par * 64:(par + 1) * 64],
                                   start=True, stop=True)
                s_z.inc(ins)
            w_vb = s_z.n

            for jj in range(NJH):
                j = b * NJH + jj
                for h in range(16):
                    @blk.vector
                    def _(e, j=j, h=h, b=b, w_vb=w_vb):
                        e.wait_ge(s_z.h, w_vb)
                        e.wait_ge(s_pr.h, w_p1)
                        s_stt.inc(e.scalar_tensor_tensor(
                            out=c.X[:, j * D + h * 64: j * D + (h + 1) * 64],
                            in0=self.pVB[:, h * 64:(h + 1) * 64],
                            scalar=self.P[:, j * 16 + h: j * 16 + h + 1],
                            in1=c.X[:, j * D + h * 64: j * D + (h + 1) * 64],
                            op0=ALU.mult, op1=ALU.add))
        self.xtile_dep = (s_stt, s_stt.n)
        self.done = (s_stt, s_stt.n)


def build_l0attn_program(T=2048, D=1024, ncore=8):
    nc = bass.Bass("TRN2", target_bir_lowering=False)
    x_dram = nc.dram_tensor("x", [T, D], F32, kind="ExternalInput").ap()
    dram = {
        "Wq": nc.dram_tensor("Wq", [64, 64], F32, kind="ExternalInput").ap(),
        "Wk": nc.dram_tensor("Wk", [64, 64], F32, kind="ExternalInput").ap(),
        "Wv": nc.dram_tensor("Wv", [64, 64], F32, kind="ExternalInput").ap(),
    }
    y_dram = nc.dram_tensor("y", [T, D], F32, kind="ExternalOutput").ap()

    with nc.Block() as blk:
        ctx = Ctx(nc, blk)
        emit_init(ctx)
        ctx.X = ctx.sb("X", [128, (T // 128) * D], F32)
        s_x = ctx.sem("f0_xload")

        @blk.sync
        def _(e):
            s_x.inc(e.dma_start(
                out=ctx.X[:].rearrange("p (j d) -> p j d", d=D),
                in_=x_dram.rearrange("(j p) d -> p j d", p=128)), 16)
        ctx.xload_count = s_x.n

        att = L0Attn(ctx, T, D, dram, [list(range(ncore))])
        emit_psum_clear(ctx, [att.pZ[0], att.pZ[1], att.pXs, att.pVB, att.pT])
        att.load_weights(ctx.init_done)
        att.emit("a")

        s_out = ctx.sem("out")

        @blk.sync
        def _(e):
            e.wait_ge(att.done[0].h, att.done[1])
            s_out.inc(e.dma_start(
                out=y_dram.rearrange("(j p) d -> p j d", p=128),
                in_=ctx.X[:].rearrange("p (j d) -> p j d", d=D)), 16)

        @blk.sync
        def _(e):
            e.wait_ge(s_out.h, s_out.n)

    return nc


class L1FFN:

    def __init__(self, ctx, T, D, dram):
        self.ctx = ctx
        self.T, self.D = T, D
        self.I, self.E = 512, 8
        self.DK = D // 128
        self.NJ = T // 128
        self.IC = self.I // 128
        self.dram = dram
        c = ctx
        nc = c.nc
        DK, NJ = self.DK, self.NJ
        self.gpB = nc.dram_tensor("m1_gpB", [8, 512, 1024], BF16)
        self.upB = nc.dram_tensor("m1_upB", [8, 512, 1024], BF16)
        self.dpB = nc.dram_tensor("m1_dpB", [8, 1024, 512], BF16)
        self.XHAT = c.sb("m1_XHAT", [128, 1024], BF16)
        self.XT = c.sb("m1_XT", [128, DK * T], BF16)
        self.GWT = c.sb("m1_GWT", [128, DK * 8], BF16)
        self.BIASB = c.sb("m1_BIASB", [128, 8], F32)
        self.IDXEPS = c.sb("m1_IDXEPS", [128, 8], F32)
        self.onesf = c.sb("m1_onesf", [1, 128], F32)
        self.biasrow = c.sb("m1_biasrow", [1, 8], F32)
        self.gwsb = c.sb("m1_gwsb", [8, 1024], BF16)
        self.Mall = c.sb("m1_Mall", [128, NJ * 8], F32)
        self.S = c.sb("m1_S", [128, 8], F32)
        self.SCR = c.sb("m1_SCR", [128, 96], F32)
        self.COLS = c.sb("m1_COLS", [128, 8], F32)
        self.GBUF = c.sb("m1_GBUF", [128, 2 * DK * 512], BF16)
        self.UBUF = c.sb("m1_UBUF", [128, 2 * DK * 512], BF16)
        self.DBUF = c.sb("m1_DBUF", [128, 2 * 8 * 512], BF16)
        self.Gt = c.sb("m1_Gt", [128, 2 * 512], BF16)
        self.Ut = c.sb("m1_Ut", [128, 2 * 512], BF16)
        self.HID = c.sb("m1_HID", [128, 4 * T], BF16)
        self.pG = [c.ps("m1_pG0", [128, 512]), c.ps("m1_pG1", [128, 512]),
                   c.ps("m1_pG2", [128, 512]), c.ps("m1_pG3", [128, 512])]
        self.pY = [c.ps("m1_pY0", [128, 512]), c.ps("m1_pY1", [128, 512])]
        self.pR = c.ps("m1_pR", [128, 512])
        self.pT = c.ps("m1_pT", [128, 512], BF16)
        self.xtile_dep = None
        self.gw1 = 0
        self.gg = 0
        self.gy = 0

    def load_weights(self, wait_init):
        c, blk = self.ctx, self.ctx.blk
        d = self.dram
        s_l = c.sem("m1_wl")
        s_pc = c.sem("m1_pc")
        s_mm = c.sem("m1_wmm")
        s_dv = c.sem("m1_wdv")

        @blk.gpsimd
        def _(e):
            s_pc.inc(e.dma_start(out=self.gpB.ap()[:, :, :], in_=d["gate_proj"][:, :, :]), 16)
            s_pc.inc(e.dma_start(out=self.upB.ap()[:, :, :], in_=d["up_proj"][:, :, :]), 16)
            s_pc.inc(e.dma_start(out=self.dpB.ap()[:, :, :], in_=d["down_proj"][:, :, :]), 16)
            s_l.inc(e.dma_start(out=self.gwsb[:, :], in_=d["gate_weight"][:, :]), 16)
            s_l.inc(e.dma_start(out=self.biasrow[:, :],
                                in_=d["e_score_correction_bias"][:, :]), 16)
            e.memset(self.onesf[:], 1.0)
            s_l.inc(e.iota(self.IDXEPS[:], [[1, 8]], channel_multiplier=0,
                           allow_small_or_imprecise_dtypes=True))
        w_l = s_l.n
        self.w_precast = (s_pc, s_pc.n)

        @blk.vector
        def _(e):
            e.wait_ge(s_l.h, w_l)
            s_dv.inc(e.tensor_scalar(out=self.IDXEPS[:], in0=self.IDXEPS[:],
                                     scalar1=1e-30, scalar2=None, op0=ALU.mult))
        w_eps = s_dv.n

        @blk.tensor
        def _(e):
            e.wait_ge(s_l.h, w_l)
            e.wait_ge(c.sem("init").h, wait_init)
            for dk in range(self.DK):
                ins = e.transpose(self.pT[0:128, dk * 8:(dk + 1) * 8],
                                  self.gwsb[:, dk * 128:(dk + 1) * 128],
                                  c.ident[0:8, 0:8])
            s_mm.inc(ins)
            s_mm.inc(e.matmul(self.pR[:, 0:8], self.onesf[:, :], self.biasrow[:, :],
                              start=True, stop=True))
        w_mm = s_mm.n

        @blk.vector
        def _(e):
            e.wait_ge(s_mm.h, w_mm)
            e.tensor_copy(out=self.GWT[:, :], in_=self.pT[0:128, 0:self.DK * 8])
            s_dv.inc(e.tensor_copy(out=self.BIASB[:, :], in_=self.pR[:, 0:8]))
        self.w_wload = (s_dv, s_dv.n)

    def _gate_tile(self, j, XTr):
        c = self.ctx
        blk = c.blk
        T, DK = self.T, self.DK
        s_mm = c.sem("m1_rmm")
        s_ac = c.sem("m1_rac")
        s_dv = c.sem("m1_rdv")
        R = self.SCR
        CO = self.COLS

        @blk.tensor
        def _(e):
            e.wait_ge(c.sem("m1_tp_ev").h, c.sem("m1_tp_ev").n)
            e.wait_ge(self.w_wload[0].h, self.w_wload[1])
            if s_dv.n:
                e.wait_ge(s_dv.h, s_dv.n)
            for dk in range(DK):
                ins = e.matmul(self.pR[:, 0:8],
                               XTr[:, dk, j * 128:(j + 1) * 128],
                               self.GWT[:, dk * 8:(dk + 1) * 8],
                               start=(dk == 0), stop=(dk == DK - 1))
            s_mm.inc(ins)
        w_mm = s_mm.n

        @blk.scalar
        def _(e):
            e.wait_ge(s_mm.h, w_mm)
            s_ac.inc(e.activation(out=self.S[:, :], in_=self.pR[:, 0:8], func=AF.Sigmoid))
        w_s = s_ac.n

        @blk.vector
        def _(e, j=j):
            def step(ins):
                s_dv.inc(ins)
                e.wait_ge(s_dv.h, s_dv.n)
            e.wait_ge(s_ac.h, w_s)
            step(e.tensor_add(R[:, 0:8], self.S[:, :], self.BIASB[:, :]))
            F0v = R[:, 0:8].rearrange("p (a b) -> p b a", b=2)
            step(e.tensor_add(R[:, 8:12], F0v[:, 0], F0v[:, 1]))
            step(e.reduce_max(CO[:, 0:1], R[:, 8:12], axis=AX.X, negate=True))
            step(e.tensor_scalar(out=R[:, 16:20], in0=R[:, 8:12], scalar1=CO[:, 0:1],
                                 scalar2=None, op0=ALU.add))
            step(e.tensor_scalar(out=R[:, 24:28], in0=R[:, 16:20], scalar1=0.0,
                                 scalar2=-1e30, op0=ALU.is_equal, op1=ALU.mult))
            step(e.tensor_add(R[:, 32:36], R[:, 8:12], R[:, 24:28]))
            step(e.reduce_max(CO[:, 1:2], R[:, 32:36], axis=AX.X, negate=True))
            step(e.tensor_scalar(out=R[:, 40:44], in0=R[:, 8:12], scalar1=CO[:, 1:2],
                                 scalar2=None, op0=ALU.add))
            step(e.tensor_scalar(out=R[:, 48:52], in0=R[:, 40:44], scalar1=0.0,
                                 scalar2=None, op0=ALU.is_ge))
            EMv = R[:, 56:64].rearrange("p (a b) -> p b a", b=2)
            e.tensor_copy(out=EMv[:, 0], in_=R[:, 48:52])
            step(e.tensor_copy(out=EMv[:, 1], in_=R[:, 48:52]))
            step(e.tensor_mul(R[:, 64:72], R[:, 0:8], R[:, 56:64]))
            step(e.tensor_sub(R[:, 64:72], R[:, 64:72], self.IDXEPS[:, :]))
            step(e.reduce_max(CO[:, 2:3], R[:, 64:72], axis=AX.X, negate=True))
            step(e.tensor_scalar(out=R[:, 72:80], in0=R[:, 64:72], scalar1=CO[:, 2:3],
                                 scalar2=None, op0=ALU.add))
            step(e.tensor_scalar(out=R[:, 72:80], in0=R[:, 72:80], scalar1=0.0,
                                 scalar2=-1e30, op0=ALU.is_equal, op1=ALU.mult))
            step(e.tensor_add(R[:, 80:88], R[:, 64:72], R[:, 72:80]))
            step(e.reduce_max(CO[:, 3:4], R[:, 80:88], axis=AX.X, negate=True))
            step(e.tensor_scalar(out=R[:, 88:96], in0=R[:, 64:72], scalar1=CO[:, 3:4],
                                 scalar2=None, op0=ALU.add))
            step(e.tensor_scalar(out=R[:, 88:96], in0=R[:, 88:96], scalar1=0.0,
                                 scalar2=None, op0=ALU.is_ge))
            step(e.tensor_mul(R[:, 64:72], R[:, 88:96], self.S[:, :]))
            step(e.reduce_sum(CO[:, 4:5], R[:, 64:72], axis=AX.X))
            step(e.tensor_scalar(out=CO[:, 5:6], in0=CO[:, 4:5], scalar1=1e-20,
                                 scalar2=None, op0=ALU.add))
            step(e.reciprocal(CO[:, 6:7], CO[:, 5:6]))
            s_dv.inc(e.tensor_scalar(out=self.Mall[:, j * 8:(j + 1) * 8], in0=R[:, 64:72],
                                     scalar1=CO[:, 6:7], scalar2=0.5,
                                     op0=ALU.mult, op1=ALU.mult))

    def emit(self, tag):
        c = self.ctx
        blk = c.blk
        T, D, DK, NJ, IC = self.T, self.D, self.DK, self.NJ, self.IC
        s_tpmm = c.sem("m1_tp_mm")
        s_tpev = c.sem("m1_tp_ev")
        s_wl2 = c.sem("m1_wstream")
        s_g = c.sem("m1_g")
        s_act = c.sem("m1_act")
        s_hid = c.sem("m1_hid")
        s_y = c.sem("m1_y")
        s_stt = c.sem("m1_stt")
        XTr = self.XT.rearrange("p (k t) -> p k t", t=T)

        for j in range(NJ):
            wl = [(c.sem("f0_xload"), c.xload_count)]
            if self.xtile_dep is not None:
                wl.append(self.xtile_dep)
            if s_tpmm.n:
                wl.append((s_tpmm, s_tpmm.n))
            ln_done = emit_ln_tile(c, c.X, j, D, self.XHAT[:, 0:D], wl, name="m1_ln")
            wl_tp = [ln_done]
            s_rdv = c.sem("m1_rdv")
            if j == 0:
                wl_tp.append((self.w_wload[0], self.w_wload[1]))
            tp_done = emit_transpose_128(c, self.XHAT[:, 0:D],
                                         lambda b, j=j: XTr[:, b * 4:(b + 1) * 4, j * 128:(j + 1) * 128],
                                         self.pT, wl_tp, name="m1_tp")
            self._gate_tile(j, XTr)

        for ei in range(8):
            par = self.gw1 % 2
            self.gw1 += 1

            @blk.sync
            def _(e, ei=ei, par=par):
                e.wait_ge(self.w_precast[0].h, self.w_precast[1])
                if self.gw1 > 2:
                    e.wait_ge(s_y.h, (self.gw1 - 2) * NJ * 2)
                for dk in range(DK):
                    s_wl2.inc(e.dma_start(
                        out=self.GBUF[:, par * DK * 512 + dk * 512: par * DK * 512 + (dk + 1) * 512],
                        in_=self.gpB.ap()[ei, :, dk * 128:(dk + 1) * 128], transpose=True), 16)
                    s_wl2.inc(e.dma_start(
                        out=self.UBUF[:, par * DK * 512 + dk * 512: par * DK * 512 + (dk + 1) * 512],
                        in_=self.upB.ap()[ei, :, dk * 128:(dk + 1) * 128], transpose=True), 16)
                for oh in range(2):
                    for ic in range(IC):
                        s_wl2.inc(e.dma_start(
                            out=self.DBUF[:, par * 8 * 512 + (ic * 2 + oh) * 512:
                                          par * 8 * 512 + (ic * 2 + oh) * 512 + 512],
                            in_=self.dpB.ap()[ei, oh * 512:(oh + 1) * 512, ic * 128:(ic + 1) * 128],
                            transpose=True), 16)
            w_we = s_wl2.n

            for ts in range(T // 512):
                for ic in range(IC):
                    pgi = (self.gg * 2) % 4
                    pui = pgi + 1
                    gg = self.gg
                    self.gg += 1

                    @blk.tensor
                    def _(e, ei=ei, ts=ts, ic=ic, par=par, pgi=pgi, pui=pui, gg=gg, w_we=w_we):
                        e.wait_ge(s_wl2.h, w_we)
                        if gg == 0 and ei == 0:
                            e.wait_ge(s_tpev.h, s_tpev.n)
                        if gg >= 2:
                            e.wait_ge(s_hid.h, gg - 1)
                        for dk in range(DK):
                            e.matmul(self.pG[pgi][:],
                                     self.GBUF[:, par * DK * 512 + dk * 512 + ic * 128:
                                               par * DK * 512 + dk * 512 + ic * 128 + 128],
                                     XTr[:, dk, ts * 512:(ts + 1) * 512],
                                     start=(dk == 0), stop=(dk == DK - 1))
                        for dk in range(DK):
                            ins = e.matmul(self.pG[pui][:],
                                           self.UBUF[:, par * DK * 512 + dk * 512 + ic * 128:
                                                     par * DK * 512 + dk * 512 + ic * 128 + 128],
                                           XTr[:, dk, ts * 512:(ts + 1) * 512],
                                           start=(dk == 0), stop=(dk == DK - 1))
                        s_g.inc(ins)
                    w_g = s_g.n

                    @blk.scalar
                    def _(e, pgi=pgi, gg=gg, w_g=w_g):
                        e.wait_ge(s_g.h, w_g)
                        s_act.inc(e.activation(out=self.Gt[:, (gg % 2) * 512:(gg % 2) * 512 + 512],
                                               in_=self.pG[pgi][:], func=AF.Silu))
                    w_si = s_act.n

                    @blk.vector
                    def _(e, pui=pui, ic=ic, ts=ts, gg=gg, w_g=w_g, w_si=w_si):
                        e.wait_ge(s_g.h, w_g)
                        s_ut = c.sem("m1_ut")
                        s_ut.inc(e.tensor_copy(out=self.Ut[:, (gg % 2) * 512:(gg % 2) * 512 + 512],
                                               in_=self.pG[pui][:]))
                        e.wait_ge(s_ut.h, s_ut.n)
                        e.wait_ge(s_act.h, w_si)
                        s_hid.inc(e.tensor_mul(self.HID[:, ic * T + ts * 512: ic * T + ts * 512 + 512],
                                               self.Gt[:, (gg % 2) * 512:(gg % 2) * 512 + 512],
                                               self.Ut[:, (gg % 2) * 512:(gg % 2) * 512 + 512]))

            w_hid_all = s_hid.n
            for tg in range(NJ):
                for oh in range(2):
                    pyi = self.gy % 2
                    gy = self.gy
                    self.gy += 1

                    @blk.tensor
                    def _(e, ei=ei, tg=tg, oh=oh, par=par, pyi=pyi, gy=gy, w_hid_all=w_hid_all):
                        e.wait_ge(s_hid.h, w_hid_all)
                        if gy >= 2:
                            e.wait_ge(s_stt.h, gy - 1)
                        for ic in range(IC):
                            ins = e.matmul(self.pY[pyi][:],
                                           self.HID[:, ic * T + tg * 128: ic * T + (tg + 1) * 128],
                                           self.DBUF[:, par * 8 * 512 + (ic * 2 + oh) * 512:
                                                     par * 8 * 512 + (ic * 2 + oh) * 512 + 512],
                                           start=(ic == 0), stop=(ic == IC - 1))
                        s_y.inc(ins)
                    w_y = s_y.n

                    @blk.vector
                    def _(e, ei=ei, tg=tg, oh=oh, pyi=pyi, w_y=w_y):
                        e.wait_ge(s_y.h, w_y)
                        s_stt.inc(e.scalar_tensor_tensor(
                            out=c.X[:, tg * D + oh * 512: tg * D + oh * 512 + 512],
                            in0=self.pY[pyi][:],
                            scalar=self.Mall[:, tg * 8 + ei: tg * 8 + ei + 1],
                            in1=c.X[:, tg * D + oh * 512: tg * D + oh * 512 + 512],
                            op0=ALU.mult, op1=ALU.add))
        self.xtile_dep = (c.sem("m1_stt"), c.sem("m1_stt").n)
        self.done = self.xtile_dep


def build_l1ffn_program(T=2048, D=1024):
    nc = bass.Bass("TRN2", target_bir_lowering=False)
    x_dram = nc.dram_tensor("x", [T, D], F32, kind="ExternalInput").ap()
    dram = {
        "gate_weight": nc.dram_tensor("gate_weight", [8, 1024], F32, kind="ExternalInput").ap(),
        "e_score_correction_bias": nc.dram_tensor("e_score_correction_bias", [1, 8], F32, kind="ExternalInput").ap(),
        "gate_proj": nc.dram_tensor("gate_proj", [8, 512, 1024], F32, kind="ExternalInput").ap(),
        "up_proj": nc.dram_tensor("up_proj", [8, 512, 1024], F32, kind="ExternalInput").ap(),
        "down_proj": nc.dram_tensor("down_proj", [8, 1024, 512], F32, kind="ExternalInput").ap(),
    }
    y_dram = nc.dram_tensor("y", [T, D], F32, kind="ExternalOutput").ap()

    with nc.Block() as blk:
        ctx = Ctx(nc, blk)
        emit_init(ctx)
        ctx.X = ctx.sb("X", [128, (T // 128) * D], F32)
        s_x = ctx.sem("f0_xload")

        @blk.sync
        def _(e):
            s_x.inc(e.dma_start(
                out=ctx.X[:].rearrange("p (j d) -> p j d", d=D),
                in_=x_dram.rearrange("(j p) d -> p j d", p=128)), 16)
        ctx.xload_count = s_x.n

        ffn = L1FFN(ctx, T, D, dram)
        emit_psum_clear(ctx, [ffn.pG[0], ffn.pG[1], ffn.pG[2], ffn.pG[3],
                              ffn.pY[0], ffn.pY[1], ffn.pR, ffn.pT])
        ffn.load_weights(ctx.init_done)
        ffn.emit("a")

        s_out = ctx.sem("out")

        @blk.sync
        def _(e):
            e.wait_ge(ffn.done[0].h, ffn.done[1])
            s_out.inc(e.dma_start(
                out=y_dram.rearrange("(j p) d -> p j d", p=128),
                in_=ctx.X[:].rearrange("p (j d) -> p j d", d=D)), 16)

        @blk.sync
        def _(e):
            e.wait_ge(s_out.h, s_out.n)

    return nc


def emit_ln_rms_tile(ctx, X, j, D, xhat_dst, WB, wait_list, name="lnr"):
    blk = ctx.blk
    if not hasattr(ctx, "ln_cols"):
        ctx.ln_cols = ctx.sb("ln_cols", [128, 8], F32)
        ctx.ln_xc = ctx.sb("ln_xc", [128, 1024], F32)
        ctx.ln_sq = ctx.sb("ln_sq", [128, 1024], F32)
    cols, xc, sq = ctx.ln_cols, ctx.ln_xc, ctx.ln_sq
    s_d = ctx.sem(name + "_d")
    s_a = ctx.sem(name + "_a")
    xs = X[:, j * D:(j + 1) * D]
    rD = 1.0 / D
    eps1, eps2 = 1e-5, 1.1920929e-7

    @blk.vector
    def _(e):
        for (s, c) in wait_list:
            e.wait_ge(s.h, c)
        s_d.inc(e.reduce_sum(cols[:, 0:1], xs, axis=AX.X))
        e.wait_ge(s_d.h, s_d.n)
        s_d.inc(e.tensor_scalar(out=cols[:, 2:3], in0=cols[:, 0:1], scalar1=rD,
                                scalar2=None, op0=ALU.mult))
        e.wait_ge(s_d.h, s_d.n)
        s_d.inc(e.tensor_scalar(out=xc[:, 0:D], in0=xs, scalar1=cols[:, 2:3],
                                scalar2=None, op0=ALU.subtract))
    w_xc = s_d.n

    @blk.scalar
    def _(e):
        e.wait_ge(s_d.h, w_xc)
        s_a.inc(e.activation(out=sq[:, 0:D], in_=xc[:, 0:D], func=AF.Square,
                             accum_out=cols[:, 1:2]))
    w_ssq = s_a.n

    @blk.vector
    def _(e):
        e.wait_ge(s_a.h, w_ssq)
        s_d.inc(e.tensor_scalar(out=cols[:, 3:4], in0=cols[:, 1:2], scalar1=rD,
                                scalar2=eps1, op0=ALU.mult, op1=ALU.add))
    w_var = s_d.n

    @blk.scalar
    def _(e):
        e.wait_ge(s_d.h, w_var)
        s_a.inc(e.activation(out=cols[:, 6:7], in_=cols[:, 3:4], func=AF.Sqrt))
    w_std = s_a.n

    @blk.vector
    def _(e):
        e.wait_ge(s_a.h, w_std)
        s_d.inc(e.reciprocal(cols[:, 4:5], cols[:, 6:7]))
        e.wait_ge(s_d.h, s_d.n)
        s_d.inc(e.tensor_scalar(out=cols[:, 7:8], in0=cols[:, 4:5], scalar1=cols[:, 4:5],
                                scalar2=None, op0=ALU.mult))
        e.wait_ge(s_d.h, s_d.n)
        s_d.inc(e.tensor_scalar(out=cols[:, 5:6], in0=cols[:, 3:4], scalar1=-eps1,
                                scalar2=cols[:, 7:8], op0=ALU.add, op1=ALU.mult))
        e.wait_ge(s_d.h, s_d.n)
        s_d.inc(e.tensor_scalar(out=cols[:, 5:6], in0=cols[:, 5:6], scalar1=eps2,
                                scalar2=None, op0=ALU.add))
    w_m2 = s_d.n

    @blk.scalar
    def _(e):
        e.wait_ge(s_d.h, w_m2)
        s_a.inc(e.activation(out=cols[:, 6:7], in_=cols[:, 5:6], func=AF.Sqrt))
    w_std2 = s_a.n

    @blk.vector
    def _(e):
        e.wait_ge(s_a.h, w_std2)
        s_d.inc(e.reciprocal(cols[:, 7:8], cols[:, 6:7]))
        e.wait_ge(s_d.h, s_d.n)
        s_d.inc(e.tensor_scalar(out=cols[:, 4:5], in0=cols[:, 4:5], scalar1=cols[:, 7:8],
                                scalar2=None, op0=ALU.mult))
        e.wait_ge(s_d.h, s_d.n)
        s_d.inc(e.tensor_scalar(out=sq[:, 0:D], in0=xc[:, 0:D], scalar1=cols[:, 4:5],
                                scalar2=None, op0=ALU.mult))
        e.wait_ge(s_d.h, s_d.n)
        s_d.inc(e.tensor_mul(xhat_dst, sq[:, 0:D], WB[:, 0:D]))
    return (s_d, s_d.n)


class L1Attn:
    R1 = 776
    STAB = 12.0

    def __init__(self, ctx, T, D, dram, rg):
        self.ctx = ctx
        self.T, self.D = T, D
        self.TG = 8192
        self.DK = D // 128
        self.NJ = T // 128
        self.dram = dram
        self.rg = rg
        c = ctx
        nc = c.nc
        self.C1 = nc.dram_tensor("a1_C1", [6208, 1024], BF16)
        self.G1 = nc.dram_tensor("a1_G1", [8 * 6208, 1024], BF16, addr_space="Shared")
        self.C2 = nc.dram_tensor("a1_C2", [256, 8192], BF16)
        self.G2 = nc.dram_tensor("a1_G2", [2048, 8192], BF16, addr_space="Shared")
        self.QSTG = nc.dram_tensor("a1_QSTG", [2048, 1024], BF16)
        self.KSTG = nc.dram_tensor("a1_KSTG", [2048, 1024], BF16)
        self.VSTG = nc.dram_tensor("a1_VSTG", [8192, 260], BF16)
        self.GSTG = nc.dram_tensor("a1_GSTG", [32, 1024], BF16)
        self.OSTG = nc.dram_tensor("a1_OSTG", [2048, 1024], BF16)
        self.BARR = nc.dram_tensor("a1_BARR", [1, 8], F32)
        self.BARRO = nc.dram_tensor("a1_BARRO", [1, 8], F32)
        self.AR = c.sb("a1_AR", [128, 40960], BF16)
        self.XHAT = c.sb("a1_XHAT", [128, 1024], BF16)
        self.WB = c.sb("a1_WB", [128, 1024], BF16)
        self.nwrow = c.sb("a1_nwrow", [1, 1024], BF16)
        self.EV = c.sb("a1_EV", [128, 1280], BF16)
        self.VL = c.sb("a1_VL", [128, 2 * 1040], BF16)
        self.GF = c.sb("a1_GF", [16, 2048], BF16)
        self.PT = c.sb("a1_PT", [128, 4 * 512], BF16)
        self.MASK = c.sb("a1_MASK", [128, 4 * 512], BF16)
        self.miota = c.sb("a1_miota", [128, 512], F32)
        self.SC8 = c.sb("a1_SC8", [128, 1], F32)
        self.CNEG = c.sb("a1_CNEG", [128, 1], F32)
        self.ONESR = c.sb("a1_ONESR", [128, 128], BF16)
        self.SER = c.sb("a1_SER", [128, 512], F32)
        self.RCR = c.sb("a1_RCR", [128, 512], F32)
        self.COEFR = c.sb("a1_COEFR", [128, 512], BF16)
        self.CBS = c.sb("a1_CBS", [128, 2 * 512], BF16)
        self.OS = c.sb("a1_OS", [128, 1024], BF16)
        self.pS = [c.ps("a1_pS%d" % i, [128, 512]) for i in range(4)]
        self.pO = [c.ps("a1_pO%d" % i, [128, 512]) for i in range(2)]
        self.pM = [c.ps("a1_pM%d" % i, [128, 512]) for i in range(2)]
        self.xtile_dep = None

    def _XT(self):
        return self.AR[:, 0:16384]

    def _WQ(self):
        return self.AR[:, 16384:24576]

    def _WKV(self):
        return self.AR[:, 24576:40960]

    def _Qb(self):
        return self.AR[:, 0:8192]

    def _Kb(self):
        return self.AR[:, 8192:16384]

    def _Vb(self):
        return self.AR[:, 16384:24704]

    def _GR(self):
        return self.AR[:, 24704:32896]

    def _Wout(self):
        return self.AR[:, 1024:9216]

    def _OWT(self):
        return self.AR[:, 0:1024]

    def load_weights(self, wait_init):
        c, blk = self.ctx, self.ctx.blk
        d = self.dram
        s_l = c.sem("a1_wl")
        s_mm = c.sem("a1_wmm")
        s_dv = c.sem("a1_wdv")

        @blk.gpsimd
        def _(e):
            s_l.inc(e.dma_start(out=self.nwrow[:, :], in_=d["norm_w"][:, :]), 16)
            e.wait_ge(c.sem("init").h, wait_init)
            e.memset(self.SC8[:], 0.125)
            e.memset(self.CNEG[:], -self.STAB)
            e.memset(self.ONESR[:], 1.0)
            s_l.inc(e.iota(self.miota[:], [[1, 512]], channel_multiplier=0,
                           allow_small_or_imprecise_dtypes=True))
        w_l = s_l.n

        @blk.vector
        def _(e):
            e.wait_ge(s_l.h, w_l)
            for r in range(4):
                s_dv.inc(e.tensor_scalar(out=self.ln_scratch(r), in0=self.miota[:],
                                         scalar1=float(r * 128), scalar2=c.piota_ap[:, 0:1],
                                         op0=ALU.subtract, op1=ALU.subtract))
                e.wait_ge(s_dv.h, s_dv.n)
                s_dv.inc(e.tensor_scalar(out=self.MASK[:, r * 512:(r + 1) * 512],
                                         in0=self.ln_scratch(r), scalar1=0.0,
                                         scalar2=None, op0=ALU.is_ge))
        w_msk = s_dv.n

        @blk.tensor
        def _(e):
            e.wait_ge(s_l.h, w_l)
            for h2 in range(2):
                ins = e.matmul(self.pM[h2][:, 0:512],
                               self.ONESR[0:1, 0:128],
                               self.nwrow[0:1, h2 * 512:(h2 + 1) * 512],
                               start=True, stop=True)
                s_mm.inc(ins)
        w_wb = s_mm.n

        @blk.vector
        def _(e):
            e.wait_ge(s_mm.h, w_wb)
            e.tensor_copy(out=self.WB[:, 0:512], in_=self.pM[0][:, 0:512])
            s_dv.inc(e.tensor_copy(out=self.WB[:, 512:1024], in_=self.pM[1][:, 0:512]))
        self.w_wload = (s_dv, s_dv.n)

    def ln_scratch(self, r):
        return self.SER[:, 0:512] if r % 2 == 0 else self.RCR[:, 0:512]

    def emit(self, tag, stop_after=None):
        c = self.ctx
        blk = c.blk
        T, D, DK, NJ, TG = self.T, self.D, self.DK, self.NJ, self.TG
        R1 = self.R1
        s_tpmm = c.sem("a1_tp_mm")
        s_tpev = c.sem("a1_tp_ev")
        s_wst = c.sem("a1_wst")
        s_pj = c.sem("a1_pj")
        s_ev = c.sem("a1_ev")
        s_c1 = c.sem("a1_c1")
        s_cc = c.sem("a1_cc")
        s_ld = c.sem("a1_ld")
        s_qk = c.sem("a1_qk")
        s_exp = c.sem("a1_exp")
        s_msk = c.sem("a1_msk")
        s_pv = c.sem("a1_pv")
        s_ep = c.sem("a1_ep")
        s_epm = c.sem("a1_epm")
        s_c2 = c.sem("a1_c2")
        s_ow = c.sem("a1_ow")
        s_wo = c.sem("a1_wo")
        s_add = c.sem("a1_add")
        XTr = self._XT().rearrange("p (k t) -> p k t", t=T)

        @blk.gpsimd
        def _(e):
            s_wst.inc(e.dma_start(
                out=self._WQ().rearrange("p (k f) -> p k f", f=1024),
                in_=self.dram["Wq"].rearrange("(k p) f -> p k f", p=128)), 16)
            s_wst.inc(e.dma_start(
                out=self._WKV().rearrange("p (k f) -> p k f", f=2048),
                in_=self.dram["Wkv"].rearrange("(k p) f -> p k f", p=128)), 16)
            s_wst.inc(e.dma_start(
                out=self.EV[0:128, 0:128].rearrange("p (k f) -> p k f", f=16),
                in_=self.dram["Wg"].rearrange("(k p) f -> p k f", p=128)), 16)
        w_wst = s_wst.n

        for j in range(NJ):
            wl = [(c.sem("f0_xload"), c.xload_count)]
            if self.xtile_dep is not None:
                wl.append(self.xtile_dep)
            if s_tpmm.n:
                wl.append((s_tpmm, s_tpmm.n))
            wl.append((self.w_wload[0], self.w_wload[1]))
            ln_done = emit_ln_rms_tile(c, c.X, j, D, self.XHAT[:, 0:D], self.WB,
                                       wl, name="a1_ln")
            wl_tp = [ln_done]
            tp_done = emit_transpose_128(c, self.XHAT[:, 0:D],
                                         lambda b, j=j: XTr[:, b * 4:(b + 1) * 4, j * 128:(j + 1) * 128],
                                         self.pS[0].bitcast(BF16)[:, 0:512], wl_tp,
                                         name="a1_tp")
        tp_all = (s_tpev, s_tpev.n)

        WQ, WKV = self._WQ(), self._WKV()
        pj_rot = 0
        pj_free = [None, None]
        ev_free = [0, 0]

        def qk_proj(kind, ec, ts):
            nonlocal pj_rot
            pm = self.pM[pj_rot % 2]
            slot = pj_rot % 2
            pj_rot += 1

            @blk.tensor
            def _(e, kind=kind, ec=ec, ts=ts, pm=pm, slot=slot):
                e.wait_ge(s_wst.h, w_wst)
                e.wait_ge(tp_all[0].h, tp_all[1])
                if pj_free[slot] is not None:
                    e.wait_ge(pj_free[slot][0].h, pj_free[slot][1])
                for dk in range(DK):
                    if kind == "q":
                        lhsT = WQ[:, dk * 1024 + ec * 128: dk * 1024 + (ec + 1) * 128]
                    else:
                        lhsT = WKV[:, dk * 2048 + ec * 128: dk * 2048 + (ec + 1) * 128]
                    ins = e.matmul(pm[:], lhsT, XTr[:, dk, ts * 512:(ts + 1) * 512],
                                   start=(dk == 0), stop=(dk == DK - 1))
                s_pj.inc(ins)
            w_mm = s_pj.n

            @blk.vector
            def _(e, kind=kind, ec=ec, ts=ts, pm=pm, w_mm=w_mm, slot=slot):
                e.wait_ge(s_pj.h, w_mm)
                if ev_free[slot]:
                    e.wait_ge(s_c1.h, ev_free[slot])
                s_ev.inc(e.tensor_copy(out=self.EV[:, 128 + slot * 512:640 + slot * 512],
                                       in_=pm[:]))
            w_ev = s_ev.n

            @blk.sync
            def _(e, kind=kind, ec=ec, ts=ts, w_ev=w_ev, slot=slot):
                e.wait_ge(s_ev.h, w_ev)
                sec = 0 if kind == "q" else 2048
                off = (sec + 2 * 128 * ec + (ts // 2)) * 1024 + (ts % 2) * 512
                dst = bass.AP(self.C1, off, [[2048, 128], [1, 512]])
                s_c1.inc(e.dma_start(out=dst,
                                     in_=self.EV[:, 128 + slot * 512:640 + slot * 512]), 16)
            ev_free[slot] = s_c1.n
            pj_free[slot] = (s_ev, s_ev.n)

        for ts in range(4):
            for ec in range(8):
                qk_proj("q", ec, ts)
            for ec in range(8):
                qk_proj("k", ec, ts)
        self._vl_c1 = [0, 0]
        for j in range(NJ):
            vslot = j % 2
            for eh in range(2):
                pm = self.pM[pj_rot % 2]
                slot = pj_rot % 2
                pj_rot += 1

                @blk.tensor
                def _(e, j=j, eh=eh, pm=pm, slot=slot):
                    e.wait_ge(s_wst.h, w_wst)
                    e.wait_ge(tp_all[0].h, tp_all[1])
                    if pj_free[slot] is not None:
                        e.wait_ge(pj_free[slot][0].h, pj_free[slot][1])
                    for dk in range(DK):
                        ins = e.matmul(pm[:],
                                       XTr[:, dk, j * 128:(j + 1) * 128],
                                       WKV[:, dk * 2048 + 1024 + eh * 512: dk * 2048 + 1024 + (eh + 1) * 512],
                                       start=(dk == 0), stop=(dk == DK - 1))
                    s_pj.inc(ins)
                w_mm = s_pj.n

                @blk.vector
                def _(e, j=j, eh=eh, pm=pm, vslot=vslot, w_mm=w_mm):
                    e.wait_ge(s_pj.h, w_mm)
                    if eh == 0:
                        if j >= 2:
                            e.wait_ge(s_c1.h, self._vl_c1[vslot])
                        e.memset(self.VL[:, vslot * 1040:(vslot + 1) * 1040], 1.0)
                        e.nop()
                    dst = bass.AP(self.VL.tensor,
                                  self.VL.offset + vslot * 1040 + eh * 8 * 65,
                                  [[2 * 1040, 128], [65, 8], [1, 64]])
                    s_ev.inc(e.tensor_copy(out=dst, in_=pm[:].rearrange("p (h f) -> p h f", f=64)))
                pj_free[slot] = (s_ev, s_ev.n)
            w_vev = s_ev.n

            @blk.sync
            def _(e, j=j, vslot=vslot, w_vev=w_vev):
                e.wait_ge(s_ev.h, w_vev)
                off = 4096 * 1024 + (j * 128) * 1040
                dst = bass.AP(self.C1, off, [[1040, 128], [1, 1040]])
                s_c1.inc(e.dma_start(
                    out=dst, in_=self.VL[:, vslot * 1040:(vslot + 1) * 1040]), 16)
            self._vl_c1[vslot] = s_c1.n
        for ts in range(4):
            pm = self.pM[pj_rot % 2]
            slot = pj_rot % 2
            pj_rot += 1

            @blk.tensor
            def _(e, ts=ts, pm=pm, slot=slot):
                e.wait_ge(s_wst.h, w_wst)
                e.wait_ge(tp_all[0].h, tp_all[1])
                if pj_free[slot] is not None:
                    e.wait_ge(pj_free[slot][0].h, pj_free[slot][1])
                for dk in range(DK):
                    ins = e.matmul(pm[0:16, 0:512],
                                   self.EV[:, dk * 16:(dk + 1) * 16],
                                   XTr[:, dk, ts * 512:(ts + 1) * 512],
                                   start=(dk == 0), stop=(dk == DK - 1))
                s_pj.inc(ins)
            w_mm = s_pj.n

            @blk.scalar
            def _(e, ts=ts, pm=pm, w_mm=w_mm):
                e.wait_ge(s_pj.h, w_mm)
                s_ev.inc(e.activation(out=self.GF[0:16, ts * 512:(ts + 1) * 512],
                                      in_=pm[0:16, 0:512], func=AF.Sigmoid))
            pj_free[slot] = (s_ev, s_ev.n)
        w_gf = s_ev.n

        @blk.sync
        def _(e):
            e.wait_ge(s_ev.h, w_gf)
            for ts in range(4):
                off = (6176 + (ts // 2)) * 1024 + (ts % 2) * 512
                dst = bass.AP(self.C1, off, [[2 * 1024, 16], [1, 512]])
                s_c1.inc(e.dma_start(out=dst,
                                     in_=self.GF[0:16, ts * 512:(ts + 1) * 512]), 16)
        w_c1 = s_c1.n
        if stop_after == "c1":
            self.done = (s_c1, w_c1)
            self.xtile_dep = self.done
            return

        from concourse.aot_env import allow_ici_all_to_all

        @blk.gpsimd
        def _(e):
            e.wait_ge(s_c1.h, w_c1)
            s_cc.inc(e.collective_compute(
                "AllGather", ALU.bypass, self.rg,
                ins=[self.C1.ap().opt()], outs=[self.G1.ap().opt()]))
            e.wait_ge(s_cc.h, s_cc.n)
            s_cc.inc(e.collective_compute(
                "AllReduce", ALU.add, self.rg,
                ins=[self.BARR.ap().opt()], outs=[self.BARRO.ap().opt()]))
        w_a2a1 = s_cc.n

        Qb, Kb, Vb, GR = self._Qb(), self._Kb(), self._Vb(), self._GR()
        pt_rot = [0, 0]
        exp_of_rot = {}
        pv_of_rot = {}
        w_pj_all = s_pj.n
        s_stg = c.sem("a1_stg")

        @blk.scalar
        def _(e):
            e.wait_ge(s_cc.h, w_a2a1)
            pid = e.snap(e.partition_id())
            hq = e.snap(pid % 4)
            cb = e.snap(pid // 4)
            dyn_q = e.snap(hq * 524288 + cb * 1024)
            src = bass.AP(self.G1, dyn_q, [[2048, 256], [6208 * 1024, 8], [1, 1024]])
            dst = bass.AP(self.QSTG, 0, [[8 * 1024, 256], [1024, 8], [1, 1024]])
            s_stg.inc(e.dma_start(out=dst, in_=src), 16)
            src = bass.AP(self.G1, dyn_q + 2048 * 1024, [[2048, 256], [6208 * 1024, 8], [1, 1024]])
            dst = bass.AP(self.KSTG, 0, [[8 * 1024, 256], [1024, 8], [1, 1024]])
            s_stg.inc(e.dma_start(out=dst, in_=src), 16)
            dyn_v = e.snap(cb * 1064960 + hq * 260 + 4096 * 1024)
            src = bass.AP(self.G1, dyn_v, [[1040, 1024], [6208 * 1024, 8], [1, 260]])
            dst = bass.AP(self.VSTG, 0, [[260, 1024], [1024 * 260, 8], [1, 260]])
            s_stg.inc(e.dma_start(out=dst, in_=src), 16)
            dyn_g = e.snap(hq * 8192 + cb * 1024 + 6176 * 1024)
            src = bass.AP(self.G1, dyn_g, [[2 * 1024, 4], [6208 * 1024, 8], [1, 1024]])
            dst = bass.AP(self.GSTG, 0, [[8 * 1024, 4], [1024, 8], [1, 1024]])
            s_stg.inc(e.dma_start(out=dst, in_=src), 16)
        w_stg = s_stg.n

        for hp in range(2):
            @blk.gpsimd
            def _(e, hp=hp):
                e.wait_ge(s_stg.h, w_stg)
                e.wait_ge(s_pj.h, w_pj_all)
                if hp == 1:
                    e.wait_ge(s_ep.h, s_ep.n)
                    e.wait_ge(s_pv.h, s_pv.n)
                    e.wait_ge(s_qk.h, s_qk.n)
                src_q = bass.AP(self.QSTG, hp * 128 * 8 * 1024,
                                [[8 * 1024, 128], [1024, 8], [1, 1024]])
                s_ld.inc(e.dma_start(out=Qb.rearrange("p (k f) -> p k f", f=1024), in_=src_q), 16)
                src_k = bass.AP(self.KSTG, hp * 128 * 8 * 1024,
                                [[8 * 1024, 128], [1024, 8], [1, 1024]])
                s_ld.inc(e.dma_start(out=Kb.rearrange("p (k f) -> p k f", f=1024), in_=src_k), 16)
                for jp in range(8):
                    src_v = bass.AP(self.VSTG, jp * 1024 * 260 + hp * 130,
                                    [[260, 128], [260 * 128, 8], [1, 130]])
                    s_ld.inc(e.dma_start(
                        out=Vb[:, jp * 1040:(jp + 1) * 1040].rearrange("p (s f) -> p s f", f=130),
                        in_=src_v), 16)
                for par in range(2):
                    src_g = bass.AP(self.GSTG, (2 * hp + par) * 8 * 1024,
                                    [[1, 1], [1024, 8], [1, 1024]])
                    dst_g = GR[64 * par:64 * par + 1, :].rearrange("p (k f) -> p k f", f=1024)
                    s_ld.inc(e.dma_start(out=dst_g, in_=src_g), 16)
            w_ld = s_ld.n

            if hp == 0:
                self._prev_os = None
                self._prev_c2 = 0
            for qt in range(16):
                nkb = 4 * qt + 4
                for kb in range(nkb):
                    rot = pt_rot[0] % 2
                    pt_rot[0] += 1
                    diag = kb >= 4 * qt

                    @blk.tensor
                    def _(e, hp=hp, qt=qt, kb=kb, rot=rot, w_ld=w_ld):
                        e.wait_ge(s_ld.h, w_ld)
                        key = (rot,)
                        if key in exp_of_rot:
                            e.wait_ge(s_exp.h, exp_of_rot[key])
                        for par in range(2):
                            base = 64 * par
                            ins = e.matmul(
                                self.pS[par * 2 + rot][:],
                                Kb[base:base + 64, kb * 128:(kb + 1) * 128],
                                Qb[base:base + 64, qt * 512:(qt + 1) * 512],
                                start=True, stop=True)
                        s_qk.inc(ins)
                    w_qk = s_qk.n

                    @blk.scalar
                    def _(e, rot=rot, w_qk=w_qk, kb=kb):
                        e.wait_ge(s_qk.h, w_qk)
                        key = (rot,)
                        if key in pv_of_rot:
                            e.wait_ge(s_pv.h, pv_of_rot[key])
                        for par in range(2):
                            ins = e.activation(
                                out=self.PT[:, (par * 2 + rot) * 512:(par * 2 + rot + 1) * 512],
                                in_=self.pS[par * 2 + rot][:], func=AF.Exp,
                                scale=self.SC8[:, 0:1], bias=self.CNEG[:, 0:1])
                        s_exp.inc(ins)
                    exp_of_rot[(rot,)] = s_exp.n
                    w_exp = s_exp.n

                    if diag:
                        r = kb - 4 * qt

                        @blk.vector
                        def _(e, rot=rot, r=r, w_exp=w_exp):
                            e.wait_ge(s_exp.h, w_exp)
                            for par in range(2):
                                sl = self.PT[:, (par * 2 + rot) * 512:(par * 2 + rot + 1) * 512]
                                ins = e.tensor_mul(sl, sl, self.MASK[:, r * 512:(r + 1) * 512])
                            s_msk.inc(ins)
                        w_gate = (s_msk, s_msk.n)
                    else:
                        w_gate = (s_exp, w_exp)

                    prev_os = self._prev_os

                    @blk.tensor
                    def _(e, hp=hp, qt=qt, kb=kb, rot=rot, w_gate=w_gate, nkb=nkb, prev_os=prev_os):
                        e.wait_ge(w_gate[0].h, w_gate[1])
                        if kb == 0 and prev_os is not None:
                            e.wait_ge(prev_os[0].h, prev_os[1])
                        for par in range(2):
                            ins = e.matmul(
                                self.pO[par][0:65, :],
                                Vb[:, kb * 130 + par * 65: kb * 130 + par * 65 + 65],
                                self.PT[:, (par * 2 + rot) * 512:(par * 2 + rot + 1) * 512],
                                start=(kb == 0), stop=(kb == nkb - 1))
                        s_pv.inc(ins)
                    pv_of_rot[(rot,)] = s_pv.n
                w_pv_last = s_pv.n

                @blk.vector
                def _(e, hp=hp, qt=qt, w_pv_last=w_pv_last):
                    e.wait_ge(s_pv.h, w_pv_last)
                    e.tensor_copy(out=self.SER[0:1, 0:512], in_=self.pO[0][64:65, :])
                    s_ep.inc(e.tensor_copy(out=self.SER[64:65, 0:512],
                                           in_=self.pO[1][64:65, :]))
                    e.wait_ge(s_ep.h, s_ep.n)
                    e.reciprocal(self.RCR[0:1, 0:512], self.SER[0:1, 0:512])
                    s_ep.inc(e.reciprocal(self.RCR[64:65, 0:512], self.SER[64:65, 0:512]))
                    e.wait_ge(s_ep.h, s_ep.n)
                    for par in range(2):
                        base = 64 * par
                        ins = e.tensor_mul(self.COEFR[base:base + 1, 0:512],
                                           self.RCR[base:base + 1, 0:512],
                                           GR[base:base + 1, qt * 512:(qt + 1) * 512])
                    s_ep.inc(ins)
                w_coef = s_ep.n

                prev_cb = getattr(self, "_prev_cb", None)

                @blk.tensor
                def _(e, hp=hp, qt=qt, w_coef=w_coef, prev_cb=prev_cb):
                    e.wait_ge(s_ep.h, w_coef)
                    if prev_cb is not None:
                        e.wait_ge(prev_cb[0].h, prev_cb[1])
                    for par in range(2):
                        base = 64 * par
                        ins = e.matmul(self.pM[par][0:64, :],
                                       self.ONESR[base:base + 1, 0:64],
                                       self.COEFR[base:base + 1, 0:512],
                                       start=True, stop=True)
                    s_epm.inc(ins)
                w_cb = s_epm.n

                prev_c2 = self._prev_c2

                @blk.vector
                def _(e, hp=hp, qt=qt, w_cb=w_cb, prev_c2=prev_c2):
                    e.wait_ge(s_epm.h, w_cb)
                    if prev_c2:
                        e.wait_ge(s_c2.h, prev_c2)
                    e.tensor_copy(out=self.CBS[0:64, 0:512], in_=self.pM[0][0:64, :])
                    s_ep.inc(e.tensor_copy(out=self.CBS[0:64, 512:1024],
                                           in_=self.pM[1][0:64, :]))
                    e.wait_ge(s_ep.h, s_ep.n)
                    for par in range(2):
                        ins = e.tensor_mul(self.OS[0:64, par * 512:(par + 1) * 512],
                                           self.pO[par][0:64, :],
                                           self.CBS[0:64, par * 512:(par + 1) * 512])
                    s_ep.inc(ins)
                w_os = s_ep.n
                self._prev_cb = (s_ep, w_os)

                @blk.sync
                def _(e, hp=hp, qt=qt, w_os=w_os):
                    e.wait_ge(s_ep.h, w_os)
                    for par in range(2):
                        off = ((2 * hp + par) * 64) * 8192 + qt * 512
                        dst = bass.AP(self.C2, off, [[8192, 64], [1, 512]])
                        s_c2.inc(e.dma_start(out=dst,
                                             in_=self.OS[0:64, par * 512:(par + 1) * 512]), 16)
                pv_of_rot[(0,)] = s_pv.n
                pv_of_rot[(1,)] = s_pv.n
                exp_of_rot[(0,)] = s_exp.n
                exp_of_rot[(1,)] = s_exp.n
                self._prev_os = (s_ep, w_os)
                self._prev_c2 = s_c2.n
        w_c2 = s_c2.n

        @blk.gpsimd
        def _(e):
            e.wait_ge(s_c2.h, w_c2)
            s_cc.inc(e.collective_compute(
                "AllGather", ALU.bypass, self.rg,
                ins=[self.C2.ap().opt()], outs=[self.G2.ap().opt()]))
            e.wait_ge(s_cc.h, s_cc.n)
            s_cc.inc(e.collective_compute(
                "AllReduce", ALU.add, self.rg,
                ins=[self.BARR.ap().opt()], outs=[self.BARRO.ap().opt()]))
        w_a2a2 = s_cc.n

        s_stg2 = c.sem("a1_stg2")

        @blk.scalar
        def _(e):
            e.wait_ge(s_cc.h, w_a2a2)
            pid2 = e.snap(e.partition_id())
            dyn_o = e.snap(pid2 * 1024)
            src = bass.AP(self.G2, dyn_o, [[8192, 2048], [1, 1024]])
            dst = bass.AP(self.OSTG, 0, [[1024, 2048], [1, 1024]])
            s_stg2.inc(e.dma_start(out=dst, in_=src), 16)
        w_stg2 = s_stg2.n

        @blk.gpsimd
        def _(e):
            e.wait_ge(s_cc.h, w_a2a2)
            s_ow.inc(e.dma_start(
                out=self._Wout().rearrange("p (k f) -> p k f", f=1024),
                in_=self.dram["Wout"].rearrange("(k p) f -> p k f", p=128)), 16)
        w_wout = s_ow.n

        OWT = self._OWT()
        for jl in range(NJ):
            b = jl // 8

            @blk.gpsimd
            def _(e, jl=jl, b=b):
                e.wait_ge(s_stg2.h, w_stg2)
                if jl >= 2:
                    e.wait_ge(s_wo.h, (jl - 1) * 2)
                src = bass.AP(self.OSTG, (1024 * b) * 1024 + (jl % 8) * 128,
                              [[1024, 128], [128 * 1024, 8], [1, 128]])
                s_ow.inc(e.dma_start(
                    out=bass.AP(OWT.tensor, OWT.offset,
                                [[40960, 128], [128, 8], [1, 128]]),
                    in_=src), 16)
            w_o = s_ow.n

            for oh in range(2):
                pm = self.pM[(jl * 2 + oh) % 2]

                @blk.tensor
                def _(e, jl=jl, oh=oh, pm=pm, w_o=w_o):
                    e.wait_ge(s_ow.h, w_o)
                    e.wait_ge(s_ow.h, w_wout)
                    gg = jl * 2 + oh
                    if gg >= 2:
                        e.wait_ge(s_add.h, gg - 1)
                    for ic in range(8):
                        ins = e.matmul(pm[:],
                                       OWT[:, ic * 128:(ic + 1) * 128],
                                       self._Wout()[:, ic * 1024 + oh * 512: ic * 1024 + oh * 512 + 512],
                                       start=(ic == 0), stop=(ic == 7))
                    s_wo.inc(ins)
                w_wo = s_wo.n

                @blk.vector
                def _(e, jl=jl, oh=oh, pm=pm, w_wo=w_wo):
                    e.wait_ge(s_wo.h, w_wo)
                    s_add.inc(e.tensor_add(
                        c.X[:, jl * D + oh * 512: jl * D + oh * 512 + 512],
                        c.X[:, jl * D + oh * 512: jl * D + oh * 512 + 512],
                        pm[:]))
        self.xtile_dep = (s_add, s_add.n)
        self.done = (s_add, s_add.n)


def build_l1attn_program(T=2048, D=1024, ncore=8, debug=False, stop_after=None):
    nc = bass.Bass("TRN2", target_bir_lowering=False)
    x_dram = nc.dram_tensor("x", [T, D], F32, kind="ExternalInput").ap()
    dram = {
        "norm_w": nc.dram_tensor("norm_w", [1, 1024], F32, kind="ExternalInput").ap(),
        "Wq": nc.dram_tensor("Wq", [1024, 1024], F32, kind="ExternalInput").ap(),
        "Wkv": nc.dram_tensor("Wkv", [1024, 2048], F32, kind="ExternalInput").ap(),
        "Wout": nc.dram_tensor("Wout", [1024, 1024], F32, kind="ExternalInput").ap(),
        "Wg": nc.dram_tensor("Wg", [1024, 16], F32, kind="ExternalInput").ap(),
    }
    y_dram = nc.dram_tensor("y", [T, D], F32, kind="ExternalOutput").ap()

    with nc.Block() as blk:
        ctx = Ctx(nc, blk)
        emit_init(ctx)
        ctx.X = ctx.sb("X", [128, (T // 128) * D], F32)
        s_x = ctx.sem("f0_xload")

        @blk.sync
        def _(e):
            s_x.inc(e.dma_start(
                out=ctx.X[:].rearrange("p (j d) -> p j d", d=D),
                in_=x_dram.rearrange("(j p) d -> p j d", p=128)), 16)
        ctx.xload_count = s_x.n

        att = L1Attn(ctx, T, D, dram, [list(range(ncore))])
        emit_psum_clear(ctx, [att.pS[0], att.pS[1], att.pS[2], att.pS[3],
                              att.pO[0], att.pO[1], att.pM[0], att.pM[1]])
        att.load_weights(ctx.init_done)
        att.emit("a", stop_after=stop_after)

        if debug:
            s_dbg = ctx.sem("dbg")

            @blk.gpsimd
            def _(e):
                e.wait_ge(att.done[0].h, att.done[1])
                for (name, shp, src) in [
                    ("d_c2", [256, 8192], bass.AP(att.C2, 0, [[8192, 256], [1, 8192]])),
                    ("d_c1f", [6208, 1024], bass.AP(att.C1, 0, [[1024, 6208], [1, 1024]])),
                    ("d_g1f", [2048, 1024], bass.AP(att.G1, 0, [[1024, 2048], [1, 1024]])),
                    ("d_qstg", [2048, 1024], bass.AP(att.QSTG, 0, [[1024, 2048], [1, 1024]])),
                    ("d_kstg", [2048, 1024], bass.AP(att.KSTG, 0, [[1024, 2048], [1, 1024]])),
                    ("d_vstg", [8192, 260], bass.AP(att.VSTG, 0, [[260, 8192], [1, 260]])),
                    ("d_gstg", [32, 1024], bass.AP(att.GSTG, 0, [[1024, 32], [1, 1024]])),
                ]:
                    dst = nc.dram_tensor(name, shp, F32, kind="ExternalOutput").ap()
                    s_dbg.inc(e.dma_start(out=dst[:, :], in_=src), 16)
                for (name, shp, src) in [
                    ("d_xt", [128, 16384], att._XT()),
                    ("d_ser", [128, 512], att.SER[:, :]),
                    ("d_os", [128, 1024], att.OS[:, :]),
                    ("d_gf", [16, 2048], att.GF[:, :]),
                ]:
                    dst = nc.dram_tensor(name, shp, F32, kind="ExternalOutput").ap()
                    s_dbg.inc(e.dma_start(out=dst[:, :], in_=src), 16)
                e.wait_ge(s_dbg.h, s_dbg.n)

        s_out = ctx.sem("out")

        @blk.sync
        def _(e):
            e.wait_ge(att.done[0].h, att.done[1])
            s_out.inc(e.dma_start(
                out=y_dram.rearrange("(j p) d -> p j d", p=128),
                in_=ctx.X[:].rearrange("p (j d) -> p j d", d=D)), 16)

        @blk.sync
        def _(e):
            e.wait_ge(s_out.h, s_out.n)

    return nc


from concourse.bass_utils import run_bass_kernel_spmd

NCORE = 8
TLOC = 2048
DM = 1024


def _shard_x(x3, i):
    return np.ascontiguousarray(
        np.concatenate([x3[0, 1024 * i:1024 * (i + 1)],
                        x3[1, 1024 * i:1024 * (i + 1)]], axis=0))


def _unshard(parts):
    y = np.zeros((2, 8192, DM), np.float32)
    for i, p in enumerate(parts):
        y[0, 1024 * i:1024 * (i + 1)] = p[0:1024]
        y[1, 1024 * i:1024 * (i + 1)] = p[1024:2048]
    return y


def kernel(L0_attn_x, L0_attn_Wq, L0_attn_Wk, L0_attn_Wv,
           L0_ffn_gate_w, L0_ffn_w1, L0_ffn_b1, L0_ffn_w2, L0_ffn_b2,
           L1_attn_norm_w, L1_attn_Wq, L1_attn_Wkv, L1_attn_Wout, L1_attn_Wg,
           L1_ffn_gate_weight, L1_ffn_e_score_correction_bias,
           L1_ffn_gate_proj, L1_ffn_up_proj, L1_ffn_down_proj):
    x = np.asarray(L0_attn_x, dtype=np.float32)
    shards = [_shard_x(x, i) for i in range(NCORE)]

    nc_f0 = build_l0ffn_program(T=TLOC, D=DM, F=4096, E=8)
    nc_a0 = build_l0attn_program(T=TLOC, D=DM, ncore=NCORE)
    nc_m1 = build_l1ffn_program(T=TLOC, D=DM)
    nc_a1 = build_l1attn_program(T=TLOC, D=DM, ncore=NCORE)

    f0_w = {"gate_w": np.asarray(L0_ffn_gate_w, np.float32),
            "w1": np.asarray(L0_ffn_w1, np.float32),
            "b1": np.asarray(L0_ffn_b1, np.float32),
            "w2": np.asarray(L0_ffn_w2, np.float32),
            "b2": np.asarray(L0_ffn_b2, np.float32)}
    a0_w = {"Wq": np.asarray(L0_attn_Wq, np.float32),
            "Wk": np.asarray(L0_attn_Wk, np.float32),
            "Wv": np.asarray(L0_attn_Wv, np.float32)}
    m1_w = {"gate_weight": np.asarray(L1_ffn_gate_weight, np.float32),
            "e_score_correction_bias": np.asarray(L1_ffn_e_score_correction_bias, np.float32).reshape(1, 8),
            "gate_proj": np.asarray(L1_ffn_gate_proj, np.float32),
            "up_proj": np.asarray(L1_ffn_up_proj, np.float32),
            "down_proj": np.asarray(L1_ffn_down_proj, np.float32)}
    a1_w = {"norm_w": np.asarray(L1_attn_norm_w, np.float32).reshape(1, 1024),
            "Wq": np.asarray(L1_attn_Wq, np.float32),
            "Wkv": np.asarray(L1_attn_Wkv, np.float32),
            "Wout": np.asarray(L1_attn_Wout, np.float32),
            "Wg": np.asarray(L1_attn_Wg, np.float32)}

    def launch(nc, wdict):
        nonlocal shards
        in_maps = [dict(wdict, x=shards[i]) for i in range(NCORE)]
        res = run_bass_kernel_spmd(nc, in_maps, core_ids=list(range(NCORE)))
        shards = [np.asarray(res.results[i]["y"], np.float32) for i in range(NCORE)]

    launch(nc_f0, f0_w)
    launch(nc_a0, a0_w)
    launch(nc_f0, f0_w)
    launch(nc_m1, m1_w)
    launch(nc_a1, a1_w)
    launch(nc_m1, m1_w)
    return _unshard(shards)
